# version 27
# baseline (speedup 1.0000x reference)
import math
import numpy as np
import ml_dtypes

import concourse.bass as bass
import concourse.mybir as mybir
from concourse.bass_utils import run_bass_kernel_spmd

F32 = mybir.dt.float32
BF16 = mybir.dt.bfloat16
AF = mybir.ActivationFunctionType
OP = mybir.AluOpType
AX = mybir.AxisListType
BNP = ml_dtypes.bfloat16

D = 1024
H = 16
NCH = 8
NEG = -1.0e30
LN_EPS = 1e-5
ENGS = ("pe", "act", "dve", "po", "sp")


def alibi_slopes(n=H):
    start = 2.0 ** (-(2.0 ** (-(math.log2(n) - 3))))
    return [start * start ** i for i in range(n)]


def split3_bf16(row64):
    hi = row64.astype(BNP)
    r1 = row64 - hi.astype(np.float64)
    lo = r1.astype(BNP)
    lo2 = (r1 - lo.astype(np.float64)).astype(BNP)
    return hi, lo, lo2


def make_consts(L):
    slopes = alibi_slopes()
    k = np.arange(L, dtype=np.float64)
    ramp3 = np.zeros((3 * H, L), dtype=BNP)
    ramp4 = np.zeros((H, L), dtype=BNP)
    for h in range(H):
        hi, lo, lo2 = split3_bf16(slopes[h] * k)
        ramp3[3 * h], ramp3[3 * h + 1], ramp3[3 * h + 2] = hi, lo, lo2
        ramp4[h] = (slopes[h] * k + 36.0).astype(BNP)
    kk = np.arange(128)[:, None]
    qq = np.arange(128)[None, :]
    maskT = np.where(kk <= qq, 0.0, NEG).astype(np.float32)
    maskT01 = np.where(kk <= qq, 1.0, 0.0).astype(BNP)
    return dict(
        ramp3=ramp3, ramp4=ramp4,
        ident=np.eye(128, dtype=np.float32),
        maskT=maskT, maskT01=maskT01,
        minus1=np.full((1, L), -1.0, dtype=BNP),
        ones_c128=np.ones((128, 1), dtype=BNP),
        ones_r128_bf=np.ones((1, 128), dtype=BNP),
        ones_r128_f=np.ones((1, 128), dtype=np.float32),
        ones_r64=np.ones((1, 64), dtype=BNP),
    )


class KB:

    def __init__(self, nc):
        self.nc = nc
        self.s, self.c = {}, {}
        for name in ("pe", "act", "dve", "po"):
            self._mksem(name)
        self.ops = None
        self.waited = {e: {} for e in ENGS}

    def _mksem(self, name):
        if name not in self.s:
            self.s[name] = self.nc.alloc_semaphore("sem_" + name)
            self.c[name] = 0

    def begin(self):
        self.ops = {e: [] for e in ENGS}

    def tok(self, name):
        self._mksem(name)
        return (name, self.c[name])

    def emit(self, eng, fn, waits=(), inc=None, inc_n=1):
        ws = []
        for w in waits:
            if w is None:
                continue
            sem, val = w
            if val <= 0 or self.waited[eng].get(sem, -1) >= val:
                continue
            self.waited[eng][sem] = val
            ws.append((self.s[sem], val))
        incs = []
        tk = None
        if inc is not None:
            self._mksem(inc)
            self.c[inc] += inc_n
            incs = [(self.s[inc], inc_n)]
            tk = (inc, self.c[inc])
        self.ops[eng].append((fn, ws, incs))
        return tk

    def wait_only(self, eng, waits):
        self.emit(eng, None, waits=waits)

    def run(self, name):
        ops, self.ops = self.ops, None
        with self.nc.Block(name) as block:
            sect = dict(pe=block.tensor, act=block.scalar, dve=block.vector,
                        po=block.gpsimd, sp=block.sync)
            for eng in ENGS:
                lst = ops[eng]
                if not lst:
                    continue

                def body(e, lst=lst):
                    for fn, ws, incs in lst:
                        for (sem, val) in ws:
                            e.wait_ge(sem, val)
                        if fn is None:
                            continue
                        inst = fn(e)
                        for (sem, n) in incs:
                            inst.then_inc(sem, n)
                sect[eng](body)


def build_core_program(L=2048, taps=(), phases=5):
    NT = L // 128
    NU = L // 512
    NQ = L // 512
    NF = L // 256
    NP = H // 2

    nc = bass.Bass("TRN2", target_bir_lowering=False, detect_race_conditions=False)
    kb = KB(nc)
    C = make_consts(L)

    din = {}
    for nm, shp in (("x", [L, D]), ("w_in", [D, 3 * D]), ("w_out", [D, D]),
                    ("b_qkv", [128, 24]), ("b_out_c", [128, 8]),
                    ("cn_w1", [D, D]), ("cn_w2", [D, 512]), ("cn_w3", [512, 256]),
                    ("cn_w4c", [128, 2]), ("cn_b1c", [128, 8]), ("cn_b2c", [128, 4]),
                    ("cn_b3c", [128, 2]), ("cn_b4", [1, 1]),
                    ("gc_w1", [D, 512]), ("gc_w2", [512, 256]), ("gc_w3c", [128, 2]),
                    ("gc_b1c", [128, 4]), ("gc_b2c", [128, 2]), ("gc_b3", [1, 1]),
                    ("adrange", [1, 1])):
        din[nm] = nc.dram_tensor(nm, shp, F32, kind="ExternalInput").ap()
    out_t = nc.dram_tensor("out", [L, D], F32, kind="ExternalOutput").ap()
    xspill = nc.dram_tensor("xspill", [128, NT * NCH * 128], F32, kind="Internal").ap()
    cst = {k: nc.inline_tensor(np.ascontiguousarray(v), "c_" + k).ap()
           for k, v in C.items()}
    tap_aps = {}
    for tname, shape, dt in taps:
        tap_aps[tname] = nc.dram_tensor(tname, shape, dt, kind="ExternalOutput").ap()

    guards = {}

    def sb(name, shape, dtype):
        g = nc.sbuf_tensor(name, shape, dtype)
        t = g.__enter__()
        guards[name] = g
        return t

    def sbfree(*names):
        for name in names:
            guards.pop(name).__exit__(None, None, None)

    def ps(name, shape):
        g = nc.psum_tensor(name, shape, F32)
        t = g.__enter__()
        guards[name] = g
        return t

    def fms(tens, t0, nt, c, p0=0, np_=128):
        return tens[p0:p0 + np_, t0:t0 + nt, c, :]

    def r3(ap2d, b=128):
        return ap2d.rearrange("p (a b) -> p a b", b=b)

    ident_s = sb("ident_s", [128, 128], F32)
    maskT_s = sb("maskT_s", [128, 128], F32)
    mask01_s = sb("mask01_s", [128, 128], BF16)
    onesc_s = sb("onesc_s", [128, 1], BF16)
    onesrb_s = sb("onesrb_s", [1, 128], BF16)
    onesrf_s = sb("onesrf_s", [1, 128], F32)
    onesr64_s = sb("onesr64_s", [1, 64], BF16)
    bqkv_s = sb("bqkv_s", [128, 24], F32)
    bksc_s = sb("bksc_s", [128, 8], F32)
    bout_s = sb("bout_s", [128, 8], F32)
    cnb1_s = sb("cnb1_s", [128, 8], F32)
    cnb2_s = sb("cnb2_s", [128, 4], F32)
    cnb3_s = sb("cnb3_s", [128, 2], F32)
    cnw4_s = sb("cnw4_s", [128, 2], BF16)
    gcb1_s = sb("gcb1_s", [128, 4], F32)
    gcb2_s = sb("gcb2_s", [128, 2], F32)
    gcw3_s = sb("gcw3_s", [128, 2], BF16)
    scal_s = sb("scal_s", [1, 16], F32)
    gcr_s = sb("gcr_s", [128, 8], BF16)
    gcp_s = sb("gcp_s", [128, 32], F32)
    gch1_s = sb("gch1_s", [128, 4], BF16)
    gch2_s = sb("gch2_s", [128, 2], BF16)
    rows_s = sb("rows_s", [1, 4096], F32)
    aux_s = sb("aux_s", [1, 1024], F32)
    rowb_s = sb("rowb_s", [1, 1024], BF16)
    lnt_s = sb("lnt_s", [128, 2048], F32)
    wscr_s = sb("wscr_s", [128, 2048], F32)
    wscr2_s = sb("wscr2_s", [128, 2048], F32)
    xb0_s = sb("xb0_s", [128, 512], BF16)
    xb1_s = sb("xb1_s", [128, 512], BF16)
    sq0_s = sb("sq0_s", [128, 512], BF16)
    sq1_s = sb("sq1_s", [128, 512], BF16)
    big_s = sb("big_s", [128, NT, NCH, 128], BF16)

    pA = ps("pA", [128, 512])
    pB = ps("pB", [128, 512])
    pO0 = ps("pO0", [128, 512])
    pO1 = ps("pO1", [128, 512])
    pR0 = ps("pR0", [128, 512])
    pR1 = ps("pR1", [128, 512])
    pRowA = ps("pRowA", [128, 512])
    pRowB = ps("pRowB", [128, 512])

    kb.begin()
    for dst, src in ((ident_s, cst["ident"]), (maskT_s, cst["maskT"]),
                     (mask01_s, cst["maskT01"]),
                     (onesc_s, cst["ones_c128"]), (onesrb_s, cst["ones_r128_bf"]),
                     (onesrf_s, cst["ones_r128_f"]), (onesr64_s, cst["ones_r64"]),
                     (bqkv_s, din["b_qkv"]), (bout_s, din["b_out_c"]),
                     (cnb1_s, din["cn_b1c"]), (cnb2_s, din["cn_b2c"]),
                     (cnb3_s, din["cn_b3c"]), (gcb1_s, din["gc_b1c"]),
                     (gcb2_s, din["gc_b2c"])):
        kb.emit("sp", lambda e, d=dst, s=src: e.dma_start(d[:], s[:]),
                inc="dC", inc_n=16)
    kb.emit("sp", lambda e: e.dma_start(scal_s[0:1, 8:9], din["cn_b4"][:]), inc="dC", inc_n=16)
    kb.emit("sp", lambda e: e.dma_start(scal_s[0:1, 9:10], din["gc_b3"][:]), inc="dC", inc_n=16)
    kb.emit("sp", lambda e: e.dma_start(scal_s[0:1, 2:3], din["adrange"][:]), inc="dC", inc_n=16)
    kb.emit("sp", lambda e: e.dma_start(wscr_s[:, 0:2], din["cn_w4c"][:]), inc="dC", inc_n=16)
    kb.emit("sp", lambda e: e.dma_start(wscr_s[:, 2:4], din["gc_w3c"][:]), inc="dC", inc_n=16)
    tall = kb.tok("dC")
    kb.emit("dve", lambda e: e.tensor_copy(cnw4_s[:], wscr_s[:, 0:2]), waits=[tall])
    kb.emit("dve", lambda e: e.tensor_copy(gcw3_s[:], wscr_s[:, 2:4]), waits=[tall])
    kb.emit("act", lambda e: e.mul(bksc_s[:], bqkv_s[:, 8:16], 0.125), waits=[tall])
    kb.emit("act", lambda e: e.mul(scal_s[0:1, 0:1], scal_s[0:1, 8:9], 0.5), waits=[tall])
    kb.emit("act", lambda e: e.mul(scal_s[0:1, 1:2], scal_s[0:1, 9:10], 0.5), waits=[tall])
    kb.run("ph0")

    wqb = (sb("wqb0", [128, 1024], BF16), sb("wqb1", [128, 1024], BF16))
    wkb = (sb("wkb0", [128, 1024], BF16), sb("wkb1", [128, 1024], BF16))
    wvb = (sb("wvb0", [128, 1024], BF16), sb("wvb1", [128, 1024], BF16))
    wq_ready = [None, None]
    wchain_state = [None, 0, None]

    def emit_wchain(cp):
        pp = cp % 2
        ct = None
        for wi, (wb, coff) in enumerate(((wqb, 0), (wkb, D), (wvb, 2 * D))):
            gi = wchain_state[1]
            wchain_state[1] += 1
            scr = (wscr_s, wscr2_s)[gi % 2]
            col = coff + cp * 128
            td = kb.emit("sp", lambda e, col=col, scr=scr: e.dma_start(
                scr[:, 0:1024].rearrange("p (k j) -> p k j", j=128),
                din["w_in"][:, col:col + 128].rearrange("(k p) j -> p k j", p=128)),
                waits=[wchain_state[0] if gi % 2 == 0 else wchain_state[2]],
                inc="dWin%d" % (gi % 2), inc_n=16)
            ct = kb.emit("po", lambda e, wb=wb, pp=pp, scr=scr: e.tensor_copy(
                wb[pp][:], scr[:, 0:1024]), waits=[td, w_free[pp]], inc="po")
            if gi % 2 == 0:
                wchain_state[0] = ct
            else:
                wchain_state[2] = ct
        wq_ready[pp] = ct

    w_free = [None, None]
    ln1T = sb("ln1T", [128, NT, NCH, 128], BF16)
    xT = sb("xT", [128, NT, NCH, 128], F32)
    land = (sb("land0", [128, 1024], F32), sb("land1", [128, 1024], F32))

    kb.begin()
    land_free = [None, None]
    tile_done = {}
    for t in range(NT):
        lb = land[t % 2]
        pp = ((pA, pB), (pO0, pO1))[t % 2]
        tdma = kb.emit("sp", lambda e, lb=lb, t=t: e.dma_start(
            lb[:], din["x"][t * 128:(t + 1) * 128, :]),
            waits=[land_free[t % 2]], inc="dL%d" % (t % 2), inc_n=16)
        for half in range(2):
            pt = pp[half]
            tp = None
            for cc in range(4):
                c = half * 4 + cc
                tp = kb.emit("pe", lambda e, pt=pt, cc=cc, lb=lb, c=c: e.transpose(
                    pt[:, cc * 128:(cc + 1) * 128], lb[:, c * 128:(c + 1) * 128],
                    ident_s[:]), waits=[tdma, tile_done.get(t - 2)], inc="pe")
            tv = kb.emit("dve", lambda e, pt=pt, t=t, half=half: e.tensor_copy(
                xT[:, t, half * 4:half * 4 + 4, :], r3(pt[:])),
                waits=[tp], inc="dve")
        tile_done[t] = tv
        land_free[t % 2] = tp
    emit_wchain(0)
    emit_wchain(1)
    spill_last = None
    ln1_done = {}
    rowA_free = rowB_free = None
    for u in range(NU):
        tx = tile_done[4 * u + 3]
        mm_s = mm_q = None
        for c in range(NCH):
            xb = (xb0_s, xb1_s)[c % 2]
            sq = (sq0_s, sq1_s)[c % 2]
            tc = kb.emit("act", lambda e, xb=xb, u=u, c=c: e.activation(
                r3(xb[:]), fms(xT, 4 * u, 4, c), AF.Copy), waits=[tx, mm_s], inc="act")
            tq = kb.emit("po", lambda e, xb=xb, sq=sq: e.tensor_tensor(
                sq[:], xb[:], xb[:], OP.mult), waits=[tc], inc="po")
            mm_s = kb.emit("pe", lambda e, xb=xb, c=c: e.matmul(
                pRowA[0:1, :], onesc_s[:], xb[:], start=(c == 0), stop=(c == NCH - 1)),
                waits=[tc, rowA_free], inc="pe")
            mm_q = kb.emit("pe", lambda e, sq=sq, c=c: e.matmul(
                pRowB[0:1, :], onesc_s[:], sq[:], start=(c == 0), stop=(c == NCH - 1)),
                waits=[tq, rowB_free], inc="pe")
        m_bf, r_bf = rowb_s[0:1, 0:512], rowb_s[0:1, 512:1024]
        a0, a1 = aux_s[0:1, 0:512], aux_s[0:1, 512:1024]
        t1 = kb.emit("dve", lambda e, m_bf=m_bf: e.tensor_scalar(
            m_bf[:], pRowA[0:1, :], 1.0 / D, None, OP.mult), waits=[mm_s], inc="dve")
        t2 = kb.emit("dve", lambda e, a0=a0, m_bf=m_bf: e.tensor_tensor(
            a0[:], m_bf[:], m_bf[:], OP.mult), waits=[t1], inc="dve")
        t3 = kb.emit("dve", lambda e, a1=a1: e.tensor_scalar(
            a1[:], pRowB[0:1, :], 1.0 / D, LN_EPS, OP.mult, OP.add), waits=[mm_q], inc="dve")
        t4 = kb.emit("dve", lambda e, a0=a0, a1=a1: e.tensor_tensor(
            a0[:], a1[:], a0[:], OP.subtract), waits=[t2, t3], inc="dve")
        rowA_free, rowB_free = t1, t3
        t5 = kb.emit("act", lambda e, a0=a0, a1=a1: e.activation(
            a1[:], a0[:], AF.Ln), waits=[t4], inc="act")
        t6 = kb.emit("act", lambda e, a1=a1, r_bf=r_bf: e.activation(
            r_bf[:], a1[:], AF.Exp, scale=-0.5), waits=[t5], inc="act")
        bm = kb.emit("pe", lambda e, m_bf=m_bf: e.matmul(
            pR0[:], onesrb_s[:], m_bf[:], start=True, stop=True),
            waits=[t1, ln1_done.get(u - 1)], inc="pe")
        br = kb.emit("pe", lambda e, r_bf=r_bf: e.matmul(
            pR1[:], onesrb_s[:], r_bf[:], start=True, stop=True),
            waits=[t6], inc="pe")
        tn = None
        for ch in range(2):
            c0 = ch * 4
            lv = lnt_s[:].rearrange("p (a b j) -> p a b j", b=4, j=128)
            bc0 = r3(pR0[:]).unsqueeze(2).broadcast_to([128, 4, 4, 128])
            bc1 = r3(pR1[:]).unsqueeze(2).broadcast_to([128, 4, 4, 128])
            tt1 = kb.emit("dve", lambda e, u=u, c0=c0, lv=lv, bc0=bc0: e.tensor_tensor(
                lv, xT[:, 4 * u:4 * u + 4, c0:c0 + 4, :], bc0, OP.subtract),
                waits=[bm, br], inc="dve")
            tn = kb.emit("dve", lambda e, u=u, c0=c0, lv=lv, bc1=bc1: e.tensor_tensor(
                ln1T[:, 4 * u:4 * u + 4, c0:c0 + 4, :], lv, bc1, OP.mult),
                waits=[tt1], inc="dve")
        ln1_done[u] = tn
        spill_last = kb.emit("sp", lambda e, u=u: e.dma_start(
            xspill[:, u * 4096:(u + 1) * 4096].rearrange(
                "p (a b c) -> p a b c", b=NCH, c=128),
            xT[:, 4 * u:4 * u + 4, :, :]), waits=[tn], inc="dSp", inc_n=16)
    kb.wait_only("sp", [kb.tok("dSp")])
    kb.run("ph1")
    if phases <= 1:
        return nc

    if "tap_ln1T" in tap_aps:
        kb.begin()
        kb.emit("po", lambda e: e.dma_start(
            tap_aps["tap_ln1T"][:].rearrange("p (t c j) -> p t c j", c=NCH, j=128),
            ln1T[:, :, :, :]), inc="dT", inc_n=16)
        kb.wait_only("po", [kb.tok("dT")])
        kb.run("ph1t")

    sbfree("land1", "land0", "xT")

    qh = ((sb("qh00", [128, L], BF16), sb("qh01", [128, L], BF16)),
          (sb("qh10", [128, L], BF16), sb("qh11", [128, L], BF16)))
    kh = ((sb("kh00", [128, L], BF16), sb("kh01", [128, L], BF16)),
          (sb("kh10", [128, L], BF16), sb("kh11", [128, L], BF16)))
    vfm = (sb("vfm0", [128, L], BF16), sb("vfm1", [128, L], BF16))
    vt = ((sb("vt00", [128, NT, 128], BF16), sb("vt01", [128, NT, 128], BF16)),
          (sb("vt10", [128, NT, 128], BF16), sb("vt11", [128, NT, 128], BF16)))
    PT = tuple(sb("pt%d" % i, [128, 512], BF16) for i in range(5))
    rsb = sb("rsb", [128, 512], BF16)
    pQK = (pR1,)

    kb.begin()
    spill_done = kb.tok("dSp")
    kb.wait_only("pe", [ln1_done[NU - 1]])
    init_dve = None
    for pr in qh:
        for t_ in pr:
            init_dve = kb.emit("dve", lambda e, t_=t_: e.memset(t_[64:67, :], 1.0),
                               waits=[spill_done], inc="dve")
    for pr in vt:
        init_dve = kb.emit("dve", lambda e, pr=pr: e.memset(pr[0][:, :, 64:65], 1.0), inc="dve")
        init_dve = kb.emit("dve", lambda e, pr=pr: e.memset(pr[1][:, :, 0:1], 1.0), inc="dve")
        init_dve = kb.emit("dve", lambda e, pr=pr: e.memset(pr[1][:, :, 1:64], 0.0), inc="dve")
    init_dg = [[None, None], [None, None]]
    for pp in range(2):
        for h01 in range(2):
            init_dg[pp][h01] = kb.emit("po", lambda e, t_=kh[pp][h01]: e.dma_start(
                t_[67:68, :], cst["minus1"][:]), waits=[spill_done],
                inc="dK%d%d" % (pp, h01), inc_n=16)

    q_free = [[None, None], [None, None]]
    v_free = [[None, None], [None, None]]
    vfm_free = [[None] * NU, [None] * NU]
    vtr_tok = {}
    pt_free = [None] * 5
    pS_free = [None, None, None]
    gkt = [0]
    pO_free = [None, None]
    pQK_free = [None, None]
    pR0_free = [None]
    rsb_free = [None]
    r_free = [None, None]
    escr_free = [None, None]
    evq = [[None], [None]]
    evk = [[None], [None]]
    evv = [[None], [None]]
    head_tok_p = [None, None]
    wscr_free = None
    pending_evac = None
    pending_pvs = []
    qkv_gi = [0]

    def qkv_filler(cp):
        pp = cp % 2
        steps = []
        for mi, wb in enumerate((wqb, wkb, wvb)):
            for u in range(NU):
                gi = qkv_gi[0]
                qkv_gi[0] += 1
                last = {}
                for kc in range(NCH):
                    def mm_step(kc=kc, gi=gi, wb=wb, pp=pp, u=u, last=last):
                        pg = pQK[0]
                        last["mm"] = kb.emit("pe", lambda e, pg=pg, wb=wb, pp=pp, kc=kc, u=u: e.matmul(
                            pg[:], wb[pp][:, kc * 128:(kc + 1) * 128],
                            fms(ln1T, 4 * u, 4, kc),
                            start=(kc == 0), stop=(kc == NCH - 1)),
                            waits=[wq_ready[pp], pQK_free[0]], inc="pe")
                    steps.append(mm_step)
                def evac_step(mi=mi, u=u, gi=gi, pp=pp, cp=cp, last=last):
                    pg = pQK[0]
                    mm = last["mm"]
                    if mi < 2:
                        dst = qh[pp] if mi == 0 else kh[pp]
                        scr = (xb0_s, xb1_s)[gi % 2]
                        if mi == 0:
                            tv = kb.emit("dve", lambda e, pg=pg, scr=scr, cp=cp: e.tensor_scalar(
                                scr[:], pg[:], bqkv_s[:, cp:cp + 1], None, OP.add),
                                waits=[mm, escr_free[gi % 2]], inc="dve")
                        else:
                            tv = kb.emit("dve", lambda e, pg=pg, scr=scr, cp=cp: e.tensor_scalar(
                                scr[:], pg[:], 0.125, bksc_s[:, cp:cp + 1],
                                OP.mult, OP.add),
                                waits=[mm, escr_free[gi % 2]], inc="dve")
                        tsh = None
                        for h01 in range(2):
                            tsh = kb.emit("po", lambda e, scr=scr, h01=h01, dst=dst, u=u: e.dma_start(
                                dst[h01][0:64, u * 512:(u + 1) * 512],
                                scr[h01 * 64:h01 * 64 + 64, :]),
                                waits=[tv, q_free[pp][h01]],
                                inc="dE%d" % (gi % 2), inc_n=16)
                        escr_free[gi % 2] = tsh
                        (evq if mi == 0 else evk)[pp][0] = tsh
                        pQK_free[0] = tv
                    else:
                        tv = kb.emit("dve", lambda e, pg=pg, pp=pp, u=u, cp=cp: e.tensor_scalar(
                            vfm[pp][:, u * 512:(u + 1) * 512], pg[:],
                            bqkv_s[:, 16 + cp:17 + cp], None, OP.add),
                            waits=[mm, vfm_free[pp][u]], inc="dve")
                        evv[pp][0] = tv
                        pQK_free[0] = tv
                        last_t = None
                        for h01 in range(2):
                            sem = "dV%d%d%d" % (pp, h01, u)
                            for tti in range(4):
                                ti = 4 * u + tti
                                c0 = 0 if h01 == 0 else 64
                                qe = "sp"
                                last_t = kb.emit(qe, lambda e, h01=h01, ti=ti, c0=c0, pp=pp: e.dma_start_transpose(
                                    vt[pp][h01][:, ti, c0:c0 + 64],
                                    vfm[pp][h01 * 64:h01 * 64 + 64, ti * 128:(ti + 1) * 128]),
                                    waits=[tv, v_free[pp][h01]],
                                    inc=sem, inc_n=16)
                            vtr_tok[(pp, h01, u)] = last_t
                        vfm_free[pp][u] = last_t
                steps.append(evac_step)
        return steps

    def bprep(cp):
        pp = cp % 2
        toks = []
        for h01 in range(2):
            h = 2 * cp + h01
            tr3 = kb.emit("po", lambda e, h=h, h01=h01, pp=pp: e.dma_start(
                kh[pp][h01][64:67, :], cst["ramp3"][3 * h:3 * h + 3, :]),
                waits=[q_free[pp][h01], evk[pp][0]], inc="dK%d%d" % (pp, h01), inc_n=16)
            tbp = kb.emit("po", lambda e, h=h, h01=h01, pp=pp: e.dma_start(
                qh[pp][h01][67:68, :], cst["ramp4"][h:h + 1, :]),
                waits=[q_free[pp][h01], evq[pp][0]], inc="dB%d%d" % (pp, h01), inc_n=16)
            toks.append((tr3, tbp))
        head_tok_p[pp] = toks

    for st in qkv_filler(0):
        st()
    bprep(0)

    for cp in range(NP):
        pp = cp % 2
        filler = qkv_filler(cp + 1) if cp + 1 < NP else []
        if cp + 2 < NP:
            emit_wchain(cp + 2)
        nblocks = 2 * NQ
        bi = 0
        for h01 in range(2):
            tr3, tbp = head_tok_p[pp][h01]
            last_mm = None
            for t in range(NQ):
                oi = (h01 * NQ + t) % 2
                pO = (pO0, pO1)[oi]
                Wh = 94.0 / alibi_slopes()[2 * cp + h01]
                kts = [4 * t + j for j in range(4)] + [
                    kt for kt in range(4 * t) if 128 * kt + 127 >= 512 * t - Wh]
                nk = len(kts)
                exps = [None] * nk
                def emit_pv(i, pO=pO, pp=pp, h01=h01, kts=None, nk=0, exps=None, t=t, oi=oi):
                    kt = kts[i]
                    j = kt - 4 * t if kt >= 4 * t else None
                    col0 = 128 * j if j is not None else 0
                    etok, pti = exps[i]
                    mrange = 65 if h01 == 0 else 128
                    tok = kb.emit("pe", lambda e, pO=pO, pti=pti, col0=col0, h01=h01, kt=kt, i=i, nk=nk, mrange=mrange, pp=pp: e.matmul(
                        pO[0:mrange, col0:512], vt[pp][h01][:, kt, 0:mrange], PT[pti][:, col0:512],
                        start=(i == 0), stop=(i == nk - 1)),
                        waits=[etok, vtr_tok[(pp, h01, kt // 4)], init_dve,
                               pO_free[oi] if i == 0 else None], inc="pe")
                    return (tok, pti)
                last_pv = None
                for i, kt in enumerate(kts):
                    si = gkt[0] % 3
                    pS = (pA, pB, pRowB)[si]
                    j = kt - 4 * t if kt >= 4 * t else None
                    col0 = 128 * j if j is not None else 0
                    mm = kb.emit("pe", lambda e, pS=pS, kt=kt, t=t, col0=col0, h01=h01, pp=pp: e.matmul(
                        pS[:, col0:512], kh[pp][h01][0:68, kt * 128:(kt + 1) * 128],
                        qh[pp][h01][0:68, t * 512 + col0:(t + 1) * 512],
                        start=True, stop=True),
                        waits=[tr3, tbp, init_dg[pp][h01], pS_free[si]],
                        inc="pe")
                    last_mm = mm
                    if i in (1, 2, 3) and pending_pvs:
                        pending_pvs.pop(0)()
                    if i == 4 and pending_evac is not None:
                        pending_evac()
                        pending_evac = None
                    pti = gkt[0] % 5
                    gkt[0] += 1
                    texp = kb.emit("act", lambda e, pS=pS, pti=pti, col0=col0: e.activation(
                        PT[pti][:, col0:512], pS[:, col0:512], AF.Exp),
                        waits=[mm, pt_free[pti]], inc="act")
                    if j is not None:
                        texp = kb.emit("dve", lambda e, pti=pti, col0=col0: e.tensor_tensor(
                            PT[pti][:, col0:col0 + 128], PT[pti][:, col0:col0 + 128],
                            mask01_s[:], OP.mult), waits=[texp], inc="dve")
                    exps[i] = (texp, pti)
                    pS_free[si] = texp
                    if i >= 2:
                        last_pv, lpti = emit_pv(i - 2, kts=kts, nk=nk, exps=exps)
                        pt_free[lpti] = last_pv
                    if filler:
                        filler.pop(0)()
                if pending_evac is not None:
                    pending_evac()
                    pending_evac = None
                tail_state = {}
                def mk_tail(ii, kts=kts, nk=nk, exps=exps, pp=pp, h01=h01, tail_state=tail_state, emit_pv=emit_pv):
                    def go():
                        tok, lpti = emit_pv(ii, kts=kts, nk=nk, exps=exps)
                        pt_free[lpti] = tok
                        tail_state["last"] = tok
                        if ii == nk - 1:
                            v_free[pp][h01] = tok
                    return go
                for ii in (nk - 2, nk - 1):
                    pending_pvs.append(mk_tail(ii))
                bi += 1
                while len(filler) > 30 + 12 * (nblocks - bi):
                    filler.pop(0)()
                srow = 64 if h01 == 0 else 0
                a_r = aux_s[0:1, 512:1024]
                r_bf = rowb_s[0:1, oi * 512:(oi + 1) * 512]
                def mk_parta(pO=pO, oi=oi, srow=srow, a_r=a_r, r_bf=r_bf,
                             tail_state=tail_state):
                    def go():
                        lpv = tail_state["last"]
                        tl = kb.emit("act", lambda e, pO=pO, srow=srow, a_r=a_r: e.activation(
                            a_r[:], pO[srow:srow + 1, 0:512], AF.Ln),
                            waits=[lpv, r_free[oi]], inc="act")
                        def _rexp(e, a_r=a_r, r_bf=r_bf):
                            with nc.allow_low_precision(reason="1/s as bf16 operand"):
                                return e.activation(r_bf[:], a_r[:], AF.Exp, scale=-1.0)
                        tail_state["tre"] = kb.emit("act", _rexp, waits=[tl], inc="act")
                    return go
                pending_pvs.append(mk_parta())
                def mk_evac(pO=pO, oi=oi, h01=h01, t=t, cp=cp, r_bf=r_bf,
                            tail_state=tail_state):
                    def go():
                        tre = tail_state["tre"]
                        tRm = kb.emit("pe", lambda e, r_bf=r_bf: e.matmul(
                            pR0[:], onesrb_s[:], r_bf[:], start=True, stop=True),
                            waits=[tre, pR0_free[0]], inc="pe")
                        r_free[oi] = tRm
                        tRe = kb.emit("dve", lambda e: e.tensor_copy(
                            rsb[:], pR0[:]), waits=[tRm, rsb_free[0]], inc="dve")
                        pR0_free[0] = tRe
                        p0 = h01 * 64
                        tev = kb.emit("dve", lambda e, pO=pO, t=t, cp=cp, p0=p0: e.tensor_tensor(
                            fms(big_s, 4 * t, 4, cp, p0, 64),
                            r3(pO[p0:p0 + 64, 0:512]), r3(rsb[p0:p0 + 64, :]), OP.mult),
                            waits=[tRe], inc="dve")
                        pO_free[oi] = tev
                        rsb_free[0] = tev
                    return go
                pending_evac = mk_evac()
            q_free[pp][h01] = last_mm
        for st in pending_pvs:
            st()
        pending_pvs = []
        for st in filler:
            st()
        if cp + 1 < NP:
            bprep(cp + 1)
    if pending_evac is not None:
        pending_evac()
        pending_evac = None
    kb.run("ph2")
    if phases <= 2:
        return nc

    if "tap_aout" in tap_aps:
        kb.begin()
        kb.emit("po", lambda e: e.dma_start(
            tap_aps["tap_aout"][:].rearrange("p (t c j) -> p t c j", c=NCH, j=128),
            big_s[:, :, :, :]), inc="dT", inc_n=16)
        kb.wait_only("po", [kb.tok("dT")])
        kb.run("ph2t")

    sbfree("rsb", "pt4", "pt3", "pt2", "pt1", "pt0",
           "vt11", "vt10", "vt01", "vt00", "vfm1", "vfm0",
           "kh11", "kh10", "kh01", "kh00", "qh11", "qh10", "qh01", "qh00",
           "ln1T", "wvb1", "wvb0", "wkb1", "wkb0", "wqb1", "wqb0")

    xT2 = sb("xT2", [128, NT, NCH, 128], F32)
    cn1b = sb("cn1b", [128, 8, 1024], BF16)
    cn2b = sb("cn2b", [128, 8, 512], BF16)
    cn3b = sb("cn3b", [128, 4, 256], BF16)
    gc1b = sb("gc1b", [128, 8, 512], BF16)
    gc2b = sb("gc2b", [128, 4, 256], BF16)
    woutb = sb("woutb", [128, 8, 1024], BF16)

    kb.begin()
    chains = [(woutb, din["w_out"], NCH, 1024), (cn1b, din["cn_w1"], 8, 1024),
              (gc1b, din["gc_w1"], 8, 512), (cn2b, din["cn_w2"], 8, 512),
              (gc2b, din["gc_w2"], 4, 256), (cn3b, din["cn_w3"], 4, 256)]
    wq_casts = {}
    gidx = 0
    cast_free = [None, None]
    for dst, wsrc, nkc, ncol in chains:
        for kc0 in range(0, nkc, 2):
            nk2 = min(2, nkc - kc0)
            scr = (wscr_s, wscr2_s)[gidx % 2]
            qe = ("sp", "po")[gidx % 2]
            td = kb.emit(qe, lambda e, wsrc=wsrc, kc0=kc0, nk2=nk2, ncol=ncol, scr=scr: e.dma_start(
                scr[:, 0:nk2 * ncol].rearrange("p (a b) -> p a b", b=ncol),
                wsrc[kc0 * 128:(kc0 + nk2) * 128, :].rearrange(
                    "(a p) b -> p a b", p=128)),
                waits=[cast_free[gidx % 2]], inc="dW%d" % (gidx % 2), inc_n=16)
            ct = kb.emit("act", lambda e, dst=dst, kc0=kc0, nk2=nk2, ncol=ncol, scr=scr: e.activation(
                dst[:, kc0:kc0 + nk2, 0:ncol],
                scr[:, 0:nk2 * ncol].rearrange("p (a b) -> p a b", b=ncol),
                AF.Copy), waits=[td], inc="act")
            cast_free[gidx % 2] = ct
            gidx += 1
        wq_casts[dst.name] = ct
    wcast = wq_casts["woutb"]
    ffnw_tok = ct
    xload = []
    for u in range(NU):
        xload.append(kb.emit("sp", lambda e, u=u: e.dma_start(
            xT2[:, 4 * u:4 * u + 4, :, :],
            xspill[:, u * 4096:(u + 1) * 4096].rearrange(
                "p (a b c) -> p a b c", b=NCH, c=128)), inc="dX", inc_n=16))
    pY_free = [None, None, None, None]
    tev_tok = {}
    for m in range(NCH):
        for u in range(NU):
            yi = (m * NU + u) % 4
            pY = (pA, pB, pO0, pO1)[yi]
            mm = None
            for kc in range(NCH):
                mm = kb.emit("pe", lambda e, pY=pY, kc=kc, m=m, u=u: e.matmul(
                    pY[:], woutb[:, kc, m * 128:(m + 1) * 128],
                    fms(big_s, 4 * u, 4, kc),
                    start=(kc == 0), stop=(kc == NCH - 1)),
                    waits=[wcast, pY_free[yi]], inc="pe")
            tev = kb.emit("dve", lambda e, pY=pY, m=m, u=u: e.scalar_tensor_tensor(
                fms(xT2, 4 * u, 4, m), r3(pY[:]), bout_s[:, m:m + 1],
                fms(xT2, 4 * u, 4, m), OP.add, OP.add),
                waits=[mm, xload[-1]], inc="dve")
            tev_tok[(m, u)] = tev
            pY_free[yi] = tev
    kb.run("ph3")
    if phases <= 3:
        return nc

    if "tap_x2" in tap_aps:
        kb.begin()
        kb.emit("po", lambda e: e.dma_start(
            tap_aps["tap_x2"][:].rearrange("p (t c j) -> p t c j", c=NCH, j=128),
            xT2[:, :, :, :]), inc="dT", inc_n=16)
        kb.wait_only("po", [kb.tok("dT")])
        kb.run("ph3t")

    sbfree("woutb")

    outT = sb("outT", [128, 8, 256], F32)
    h1b = sb("h1b", [128, 8, 256], BF16)
    h2b = sb("h2b", [128, 4, 256], BF16)
    h3b = sb("h3b", [128, 2, 256], BF16)
    osc = (wscr_s[:, 0:1024], wscr2_s[:, 0:1024])

    kb.begin()
    rowA_free = rowB_free = None
    ln2_done = {}
    ln2_rows = {}
    gc_last = None
    for u in range(NU):
        mm_s = mm_q = None
        for c in range(NCH):
            xb = (xb0_s, xb1_s)[c % 2]
            sq = (sq0_s, sq1_s)[c % 2]
            tc = kb.emit("act", lambda e, xb=xb, u=u, c=c: e.activation(
                r3(xb[:]), fms(xT2, 4 * u, 4, c), AF.Copy),
                waits=[mm_s, tev_tok[(c, u)]], inc="act")
            tq = kb.emit("po", lambda e, xb=xb, sq=sq: e.tensor_tensor(
                sq[:], xb[:], xb[:], OP.mult), waits=[tc], inc="po")
            mm_s = kb.emit("pe", lambda e, xb=xb, c=c: e.matmul(
                pRowA[0:1, :], onesc_s[:], xb[:], start=(c == 0), stop=(c == NCH - 1)),
                waits=[tc, rowA_free], inc="pe")
            mm_q = kb.emit("pe", lambda e, sq=sq, c=c: e.matmul(
                pRowB[0:1, :], onesc_s[:], sq[:], start=(c == 0), stop=(c == NCH - 1)),
                waits=[tq, rowB_free], inc="pe")
        m_f = rows_s[0:1, u * 512:(u + 1) * 512]
        r_f = rows_s[0:1, 2048 + u * 512:2048 + (u + 1) * 512]
        a0, a1 = aux_s[0:1, 0:512], aux_s[0:1, 512:1024]
        t1 = kb.emit("dve", lambda e, m_f=m_f: e.tensor_scalar(
            m_f[:], pRowA[0:1, :], 1.0 / D, None, OP.mult), waits=[mm_s], inc="dve")
        t2 = kb.emit("dve", lambda e, a0=a0, m_f=m_f: e.tensor_tensor(
            a0[:], m_f[:], m_f[:], OP.mult), waits=[t1], inc="dve")
        t3 = kb.emit("dve", lambda e, a1=a1: e.tensor_scalar(
            a1[:], pRowB[0:1, :], 1.0 / D, LN_EPS, OP.mult, OP.add), waits=[mm_q], inc="dve")
        t4 = kb.emit("dve", lambda e, a0=a0, a1=a1: e.tensor_tensor(
            a0[:], a1[:], a0[:], OP.subtract), waits=[t2, t3], inc="dve")
        rowA_free, rowB_free = t1, t3
        t5 = kb.emit("act", lambda e, a0=a0, a1=a1: e.activation(
            a1[:], a0[:], AF.Ln), waits=[t4], inc="act")
        t6 = kb.emit("act", lambda e, a1=a1, r_f=r_f: e.activation(
            r_f[:], a1[:], AF.Exp, scale=-0.5), waits=[t5], inc="act")
        ln2_rows[u] = (t1, t6)
        bm = kb.emit("pe", lambda e, m_f=m_f: e.matmul(
            pR0[:], onesrf_s[:], m_f[:], start=True, stop=True),
            waits=[t1, ln2_done.get(u - 1)], inc="pe")
        br = kb.emit("pe", lambda e, r_f=r_f: e.matmul(
            pR1[:], onesrf_s[:], r_f[:], start=True, stop=True),
            waits=[t6], inc="pe")
        tn = None
        for ch in range(2):
            c0 = ch * 4
            lv = lnt_s[:].rearrange("p (a b j) -> p a b j", b=4, j=128)
            bc0 = r3(pR0[:]).unsqueeze(2).broadcast_to([128, 4, 4, 128])
            bc1 = r3(pR1[:]).unsqueeze(2).broadcast_to([128, 4, 4, 128])
            tt1 = kb.emit("dve", lambda e, u=u, c0=c0, lv=lv, bc0=bc0: e.tensor_tensor(
                lv, xT2[:, 4 * u:4 * u + 4, c0:c0 + 4, :], bc0, OP.subtract),
                waits=[bm, br], inc="dve")
            tn = kb.emit("dve", lambda e, u=u, c0=c0, lv=lv, bc1=bc1: e.tensor_tensor(
                big_s[:, 4 * u:4 * u + 4, c0:c0 + 4, :], lv, bc1, OP.mult),
                waits=[tt1], inc="dve")
        gc_last = kb.emit("dve", lambda e, u=u: e.tensor_reduce(
            gcp_s[:, u * 8:(u + 1) * 8],
            big_s[:, 4 * u:4 * u + 4, :, :].transpose([0, 2, 1, 3]),
            AX.XY, OP.add), waits=[tn], inc="dve")
        ln2_done[u] = tn
    tgs = gc_last
    for u in range(1, NU):
        tgs = kb.emit("dve", lambda e, u=u: e.tensor_tensor(
            gcp_s[:, 0:8], gcp_s[:, 0:8], gcp_s[:, u * 8:u * 8 + 8], OP.add),
            waits=[tgs], inc="dve")
    tgr = kb.emit("dve", lambda e: e.tensor_scalar(
        gcr_s[:], gcp_s[:, 0:8], 1.0 / L, None, OP.mult), waits=[tgs], inc="dve")
    gcw_tok = wq_casts["gc2b"]
    mms = []
    for m in range(4):
        mm = None
        for kc in range(NCH):
            mm = kb.emit("pe", lambda e, m=m, kc=kc: e.matmul(
                pO0[:, m:m + 1], gc1b[:, kc, m * 128:(m + 1) * 128],
                gcr_s[:, kc:kc + 1], start=(kc == 0), stop=(kc == NCH - 1)),
                waits=[tgr, gcw_tok], inc="pe")
        mms.append(mm)
    g1 = []
    for m in range(4):
        g1.append(kb.emit("act", lambda e, m=m: e.activation(
            gch1_s[:, m:m + 1], pO0[:, m:m + 1], AF.Gelu,
            bias=gcb1_s[:, m:m + 1]), waits=[mms[-1]], inc="act"))
    mms = []
    for m in range(2):
        mm = None
        for kc in range(4):
            mm = kb.emit("pe", lambda e, m=m, kc=kc: e.matmul(
                pO1[:, m:m + 1], gc2b[:, kc, m * 128:(m + 1) * 128],
                gch1_s[:, kc:kc + 1], start=(kc == 0), stop=(kc == 3)),
                waits=[g1[-1]], inc="pe")
        mms.append(mm)
    g2 = []
    for m in range(2):
        g2.append(kb.emit("act", lambda e, m=m: e.activation(
            gch2_s[:, m:m + 1], pO1[:, m:m + 1], AF.Gelu,
            bias=gcb2_s[:, m:m + 1]), waits=[mms[-1]], inc="act"))
    gc_pO_tok = (g1[-1], g2[-1])
    mm = None
    for kc in range(2):
        mm = kb.emit("pe", lambda e, kc=kc: e.matmul(
            pRowA[0:1, 0:1], gcw3_s[:, kc:kc + 1], gch2_s[:, kc:kc + 1],
            start=(kc == 0), stop=(kc == 1)), waits=[g2[-1]], inc="pe")
    tth_gc = kb.emit("act", lambda e: e.activation(
        scal_s[0:1, 3:4], pRowA[0:1, 0:1], AF.Tanh, scale=0.5,
        bias=scal_s[0:1, 1:2]), waits=[mm], inc="act")
    tt = kb.emit("dve", lambda e: e.tensor_tensor(
        scal_s[0:1, 7:8], scal_s[0:1, 2:3], scal_s[0:1, 3:4], OP.mult),
        waits=[tth_gc], inc="dve")
    c0_tok = kb.emit("dve", lambda e: e.tensor_scalar(
        scal_s[0:1, 4:5], scal_s[0:1, 7:8], 0.7, 0.1, OP.mult, OP.add),
        waits=[tt], inc="dve")
    c1_tok = kb.emit("dve", lambda e: e.tensor_scalar(
        scal_s[0:1, 5:6], scal_s[0:1, 2:3], 0.3, None, OP.mult), inc="dve")
    outT2 = (outT, sb("outT2", [128, 8, 256], F32))
    pC_free = [None, None]
    pCP = (pR0, pR1)
    pC4 = (pRowA, pRowB)
    cp_free = [ln2_done[NU - 1], ln2_done[NU - 1]]
    exit_free = [gc_pO_tok[0], gc_pO_tok[1]]
    c4_free = [tth_gc, rowB_free]
    prev_tcp = [None, None]
    osc_free = [None, None]
    last_twists = [None, None]
    for fp in range(0, NF, 2):
        out0_tok = [None, None]
        for p in range(2):
            f = fp + p
            t0 = f * 2
            m_f = rows_s[0:1, f * 256:(f + 1) * 256]
            r_f = rows_s[0:1, 2048 + f * 256:2048 + (f + 1) * 256]
            rt = ln2_rows[f // 2]
            bm = kb.emit("pe", lambda e, m_f=m_f: e.matmul(
                pR0[:, 0:256], onesrf_s[:], m_f[:], start=True, stop=True),
                waits=[cp_free[0], out0_tok[0] if p else None, rt[0], rt[1],
                       last_twists[0], last_twists[1]], inc="pe")
            br = kb.emit("pe", lambda e, r_f=r_f: e.matmul(
                pR1[:, 0:256], onesrf_s[:], r_f[:], start=True, stop=True),
                waits=[cp_free[1]], inc="pe")
            lv = lnt_s[:].rearrange("p (c a j) -> p c a j", a=2, j=128)
            bc0 = pR0[:, 0:256].rearrange("p (a j) -> p a j", j=128).unsqueeze(
                1).broadcast_to([128, 8, 2, 128])
            bc1 = pR1[:, 0:256].rearrange("p (a j) -> p a j", j=128).unsqueeze(
                1).broadcast_to([128, 8, 2, 128])
            tt1 = kb.emit("dve", lambda e, t0=t0, lv=lv, bc0=bc0: e.tensor_tensor(
                lv, xT2[:, t0:t0 + 2, :, :].transpose([0, 2, 1, 3]), bc0,
                OP.subtract), waits=[bm, br], inc="dve")
            tn = kb.emit("dve", lambda e, p=p, lv=lv, bc1=bc1: e.tensor_tensor(
                outT2[p][:, :, :].rearrange("p c (a j) -> p c a j", j=128),
                lv, bc1, OP.mult), waits=[tt1], inc="dve")
            out0_tok[p] = tn
        twist_tok = [out0_tok[0], out0_tok[1]]
        for cyc in range(3):
            for p in range(2):
                f = fp + p
                t0 = f * 2
                oT = outT2[p]
                g1 = []
                for m in range(NCH):
                    yi = m % 2
                    pC = (pA, pB)[yi]
                    mm = None
                    for kc in range(NCH):
                        mm = kb.emit("pe", lambda e, pC=pC, m=m, kc=kc, t0=t0: e.matmul(
                            pC[:, 0:256], cn1b[:, kc, m * 128:(m + 1) * 128],
                            fms(big_s, t0, 2, kc),
                            start=(kc == 0), stop=(kc == NCH - 1)),
                            waits=[ffnw_tok, pC_free[yi], twist_tok[p],
                                   ln2_done[t0 // 4] if cyc == 0 else None], inc="pe")
                    g1.append(kb.emit("act", lambda e, pC=pC, m=m: e.activation(
                        h1b[:, m, :], pC[:, 0:256], AF.Gelu, bias=cnb1_s[:, m:m + 1]),
                        waits=[mm], inc="act"))
                    pC_free[yi] = g1[-1]
                g2 = []
                for m in range(4):
                    yi = m % 2
                    pC = (pA, pB)[yi]
                    mm = None
                    for kc in range(NCH):
                        mm = kb.emit("pe", lambda e, pC=pC, m=m, kc=kc: e.matmul(
                            pC[:, 0:256], cn2b[:, kc, m * 128:(m + 1) * 128],
                            h1b[:, kc, :], start=(kc == 0), stop=(kc == NCH - 1)),
                            waits=[g1[kc], pC_free[yi]], inc="pe")
                    g2.append(kb.emit("act", lambda e, pC=pC, m=m: e.activation(
                        h2b[:, m, :], pC[:, 0:256], AF.Gelu, bias=cnb2_s[:, m:m + 1]),
                        waits=[mm], inc="act"))
                    pC_free[yi] = g2[-1]
                g3 = []
                for m in range(2):
                    yi = m % 2
                    pC = (pA, pB)[yi]
                    mm = None
                    for kc in range(4):
                        mm = kb.emit("pe", lambda e, pC=pC, m=m, kc=kc: e.matmul(
                            pC[:, 0:256], cn3b[:, kc, m * 128:(m + 1) * 128],
                            h2b[:, kc, :], start=(kc == 0), stop=(kc == 3)),
                            waits=[g2[kc], pC_free[yi]], inc="pe")
                    g3.append(kb.emit("act", lambda e, pC=pC, m=m: e.activation(
                        h3b[:, m, :], pC[:, 0:256], AF.Gelu, bias=cnb3_s[:, m:m + 1]),
                        waits=[mm], inc="act"))
                    pC_free[yi] = g3[-1]
                mm = None
                for kc in range(2):
                    mm = kb.emit("pe", lambda e, kc=kc, p=p: e.matmul(
                        pC4[p][0:1, 0:256], cnw4_s[:, kc:kc + 1], h3b[:, kc, :],
                        start=(kc == 0), stop=(kc == 1)),
                        waits=[g3[kc], c4_free[p]], inc="pe")
                a_th = aux_s[0:1, p * 256:(p + 1) * 256]
                a_cp = aux_s[0:1, 512 + p * 256:512 + (p + 1) * 256]
                tth = kb.emit("act", lambda e, p=p, a_th=a_th: e.activation(
                    a_th[:], pC4[p][0:1, 0:256], AF.Tanh, scale=0.5,
                    bias=scal_s[0:1, 0:1]), waits=[mm, prev_tcp[p]], inc="act")
                c4_free[p] = tth
                tcp = kb.emit("dve", lambda e, a_th=a_th, a_cp=a_cp: e.tensor_scalar(
                    a_cp[:], a_th[:], scal_s[0:1, 5:6],
                    scal_s[0:1, 4:5], OP.mult, OP.add),
                    waits=[tth, c0_tok, c1_tok], inc="dve")
                prev_tcp[p] = tcp
                tbc = kb.emit("pe", lambda e, p=p, a_cp=a_cp: e.matmul(
                    pCP[p][:, 0:256], onesrf_s[:], a_cp[:],
                    start=True, stop=True), waits=[tcp, cp_free[p]], inc="pe")
                lA = lnt_s[:, 0:1024].rearrange("p (c f) -> p c f", f=256)
                lB = lnt_s[:, 1024:2048].rearrange("p (c f) -> p c f", f=256)
                cbc = pCP[p][:, 0:256].unsqueeze(1).broadcast_to([128, 4, 256])
                tb1 = kb.emit("dve", lambda e, oT=oT, lA=lA, cbc=cbc: e.tensor_tensor(
                    lA, oT[:, 4:8, :], cbc, OP.mult), waits=[tbc], inc="dve")
                tb2 = kb.emit("dve", lambda e, oT=oT, lB=lB, cbc=cbc: e.tensor_tensor(
                    lB, oT[:, 0:4, :], cbc, OP.mult), inc="dve")
                kb.emit("dve", lambda e, oT=oT, lA=lA: e.tensor_tensor(
                    oT[:, 0:2, :], oT[:, 0:2, :], lA[:, 0:2, :], OP.add),
                    waits=[tb1], inc="dve")
                kb.emit("dve", lambda e, oT=oT, lA=lA: e.tensor_tensor(
                    oT[:, 2:4, :], oT[:, 2:4, :], lA[:, 2:4, :], OP.subtract),
                    inc="dve")
                kb.emit("dve", lambda e, oT=oT, lB=lB: e.tensor_tensor(
                    oT[:, 4:6, :], oT[:, 4:6, :], lB[:, 0:2, :], OP.subtract),
                    waits=[tb2], inc="dve")
                tw = kb.emit("dve", lambda e, oT=oT, lB=lB: e.tensor_tensor(
                    oT[:, 6:8, :], oT[:, 6:8, :], lB[:, 2:4, :], OP.add),
                    inc="dve")
                cp_free[p] = tw
                twist_tok[p] = tw
                if cyc < 2:
                    tcast = kb.emit("dve", lambda e, t0=t0, oT=oT: e.tensor_copy(
                        big_s[:, t0:t0 + 2, :, :],
                        oT[:, :, :].rearrange("p c (a j) -> p c a j", j=128
                                              ).transpose([0, 2, 1, 3])),
                        waits=[tw, g1[-1]], inc="dve")
                    twist_tok[p] = tcast
        last_twists = [twist_tok[0], twist_tok[1]]
        for p in range(2):
            f = fp + p
            t0 = f * 2
            oT = outT2[p]
            tadd = kb.emit("dve", lambda e, t0=t0, oT=oT: e.tensor_tensor(
                xT2[:, t0:t0 + 2, :, :], xT2[:, t0:t0 + 2, :, :],
                oT[:, :, :].rearrange("p c (a j) -> p c a j", j=128
                                      ).transpose([0, 2, 1, 3]), OP.add),
                waits=[twist_tok[p]], inc="dve")
            last_twists[p] = tadd
            for s in range(2):
                t = t0 + s
                ob = osc[t % 2]
                tv = None
                for half in range(2):
                    pt_ = (pO0, pO1)[half]
                    tp = None
                    for cc in range(4):
                        c = half * 4 + cc
                        tp = kb.emit("pe", lambda e, pt_=pt_, cc=cc, t=t, c=c: e.transpose(
                            pt_[:, cc * 128:(cc + 1) * 128], xT2[:, t, c, :], ident_s[:]),
                            waits=[tadd, exit_free[half], osc_free[t % 2]], inc="pe")
                    tv = kb.emit("act", lambda e, pt_=pt_, ob=ob, half=half: e.activation(
                        ob[:, half * 512:(half + 1) * 512], pt_[:], AF.Copy),
                        waits=[tp], inc="act")
                    exit_free[half] = tv
                td = kb.emit("po", lambda e, ob=ob, t=t: e.dma_start(
                    out_t[t * 128:(t + 1) * 128, :], ob),
                    waits=[tv], inc="dO%d" % (t % 2), inc_n=16)
                osc_free[t % 2] = td
    kb.wait_only("po", [kb.tok("dO0"), kb.tok("dO1")])
    kb.run("ph45")

    for eng in ENGS:
        for sem, val in kb.waited[eng].items():
            assert val <= kb.c[sem], (eng, sem, val, kb.c[sem])
    for name in list(guards):
        try:
            guards.pop(name).__exit__(None, None, None)
        except Exception:
            pass
    return nc


def prep_inputs(inp):
    f = lambda a: np.ascontiguousarray(np.asarray(a, dtype=np.float32))
    cc = np.ascontiguousarray
    d = {}
    d["w_in"] = f(inp["L0_attn_w_in"])
    d["w_out"] = f(inp["L0_attn_w_out"])
    d["b_qkv"] = cc(f(inp["L0_attn_b_in"]).reshape(24, 128).T)
    d["b_out_c"] = cc(f(inp["L0_attn_b_out"]).reshape(8, 128).T)
    d["cn_w1"] = f(inp["L0_ffn_cn_w1"])
    d["cn_w2"] = f(inp["L0_ffn_cn_w2"])
    d["cn_w3"] = f(inp["L0_ffn_cn_w3"])
    d["cn_w4c"] = cc(f(inp["L0_ffn_cn_w4"]).reshape(2, 128).T)
    d["cn_b1c"] = cc(f(inp["L0_ffn_cn_b1"]).reshape(8, 128).T)
    d["cn_b2c"] = cc(f(inp["L0_ffn_cn_b2"]).reshape(4, 128).T)
    d["cn_b3c"] = cc(f(inp["L0_ffn_cn_b3"]).reshape(2, 128).T)
    d["cn_b4"] = f(inp["L0_ffn_cn_b4"]).reshape(1, 1)
    d["gc_w1"] = f(inp["L0_ffn_gc_w1"])
    d["gc_w2"] = f(inp["L0_ffn_gc_w2"])
    d["gc_w3c"] = cc(f(inp["L0_ffn_gc_w3"]).reshape(2, 128).T)
    d["gc_b1c"] = cc(f(inp["L0_ffn_gc_b1"]).reshape(4, 128).T)
    d["gc_b2c"] = cc(f(inp["L0_ffn_gc_b2"]).reshape(2, 128).T)
    d["gc_b3"] = f(inp["L0_ffn_gc_b3"]).reshape(1, 1)
    d["adrange"] = f(inp["L0_ffn_adaptive_range"]).reshape(1, 1)
    return d


def kernel(**inputs):
    x = np.asarray(inputs["L0_attn_x"], dtype=np.float32)
    B, L, _ = x.shape
    shared = prep_inputs(inputs)
    nc = build_core_program(L=L)
    in_maps = []
    for b in range(B):
        m = dict(shared)
        m["x"] = np.ascontiguousarray(x[b])
        in_maps.append(m)
    res = run_bass_kernel_spmd(nc, in_maps, core_ids=list(range(B)))
    return np.stack([np.asarray(r["out"]) for r in res.results], axis=0)


# version 28
# speedup vs baseline: 1.0045x; 1.0045x over previous
import math
import numpy as np
import ml_dtypes

import concourse.bass as bass
import concourse.mybir as mybir
from concourse.bass_utils import run_bass_kernel_spmd

F32 = mybir.dt.float32
BF16 = mybir.dt.bfloat16
AF = mybir.ActivationFunctionType
OP = mybir.AluOpType
AX = mybir.AxisListType
BNP = ml_dtypes.bfloat16

D = 1024
H = 16
NCH = 8
NEG = -1.0e30
LN_EPS = 1e-5
ENGS = ("pe", "act", "dve", "po", "sp")


def alibi_slopes(n=H):
    start = 2.0 ** (-(2.0 ** (-(math.log2(n) - 3))))
    return [start * start ** i for i in range(n)]


def split3_bf16(row64):
    hi = row64.astype(BNP)
    r1 = row64 - hi.astype(np.float64)
    lo = r1.astype(BNP)
    lo2 = (r1 - lo.astype(np.float64)).astype(BNP)
    return hi, lo, lo2


def make_consts(L):
    slopes = alibi_slopes()
    k = np.arange(L, dtype=np.float64)
    ramp3 = np.zeros((3 * H, L), dtype=BNP)
    ramp4 = np.zeros((H, L), dtype=BNP)
    for h in range(H):
        hi, lo, lo2 = split3_bf16(slopes[h] * k)
        ramp3[3 * h], ramp3[3 * h + 1], ramp3[3 * h + 2] = hi, lo, lo2
        ramp4[h] = (slopes[h] * k + 36.0).astype(BNP)
    kk = np.arange(128)[:, None]
    qq = np.arange(128)[None, :]
    maskT = np.where(kk <= qq, 0.0, NEG).astype(np.float32)
    maskT01 = np.where(kk <= qq, 1.0, 0.0).astype(BNP)
    return dict(
        ramp3=ramp3, ramp4=ramp4,
        ident=np.eye(128, dtype=np.float32),
        maskT=maskT, maskT01=maskT01,
        minus1=np.full((1, L), -1.0, dtype=BNP),
        ones_c128=np.ones((128, 1), dtype=BNP),
        ones_r128_bf=np.ones((1, 128), dtype=BNP),
        ones_r128_f=np.ones((1, 128), dtype=np.float32),
        ones_r64=np.ones((1, 64), dtype=BNP),
    )


class KB:

    def __init__(self, nc):
        self.nc = nc
        self.s, self.c = {}, {}
        for name in ("pe", "act", "dve", "po"):
            self._mksem(name)
        self.ops = None
        self.waited = {e: {} for e in ENGS}

    def _mksem(self, name):
        if name not in self.s:
            self.s[name] = self.nc.alloc_semaphore("sem_" + name)
            self.c[name] = 0

    def begin(self):
        self.ops = {e: [] for e in ENGS}

    def tok(self, name):
        self._mksem(name)
        return (name, self.c[name])

    def emit(self, eng, fn, waits=(), inc=None, inc_n=1):
        ws = []
        for w in waits:
            if w is None:
                continue
            sem, val = w
            if val <= 0 or self.waited[eng].get(sem, -1) >= val:
                continue
            self.waited[eng][sem] = val
            ws.append((self.s[sem], val))
        incs = []
        tk = None
        if inc is not None:
            self._mksem(inc)
            self.c[inc] += inc_n
            incs = [(self.s[inc], inc_n)]
            tk = (inc, self.c[inc])
        self.ops[eng].append((fn, ws, incs))
        return tk

    def wait_only(self, eng, waits):
        self.emit(eng, None, waits=waits)

    def run(self, name):
        ops, self.ops = self.ops, None
        with self.nc.Block(name) as block:
            sect = dict(pe=block.tensor, act=block.scalar, dve=block.vector,
                        po=block.gpsimd, sp=block.sync)
            for eng in ENGS:
                lst = ops[eng]
                if not lst:
                    continue

                def body(e, lst=lst):
                    for fn, ws, incs in lst:
                        for (sem, val) in ws:
                            e.wait_ge(sem, val)
                        if fn is None:
                            continue
                        inst = fn(e)
                        for (sem, n) in incs:
                            inst.then_inc(sem, n)
                sect[eng](body)


def build_core_program(L=2048, taps=(), phases=5):
    NT = L // 128
    NU = L // 512
    NQ = L // 512
    NF = L // 256
    NP = H // 2

    nc = bass.Bass("TRN2", target_bir_lowering=False, detect_race_conditions=False)
    kb = KB(nc)
    C = make_consts(L)

    din = {}
    for nm, shp in (("x", [L, D]), ("w_in", [D, 3 * D]), ("w_out", [D, D]),
                    ("b_qkv", [128, 24]), ("b_out_c", [128, 8]),
                    ("cn_w1", [D, D]), ("cn_w2", [D, 512]), ("cn_w3", [512, 256]),
                    ("cn_w4c", [128, 2]), ("cn_b1c", [128, 8]), ("cn_b2c", [128, 4]),
                    ("cn_b3c", [128, 2]), ("cn_b4", [1, 1]),
                    ("gc_w1", [D, 512]), ("gc_w2", [512, 256]), ("gc_w3c", [128, 2]),
                    ("gc_b1c", [128, 4]), ("gc_b2c", [128, 2]), ("gc_b3", [1, 1]),
                    ("adrange", [1, 1])):
        din[nm] = nc.dram_tensor(nm, shp, F32, kind="ExternalInput").ap()
    out_t = nc.dram_tensor("out", [L, D], F32, kind="ExternalOutput").ap()
    xspill = nc.dram_tensor("xspill", [128, NT * NCH * 128], F32, kind="Internal").ap()
    cst = {k: nc.inline_tensor(np.ascontiguousarray(v), "c_" + k).ap()
           for k, v in C.items()}
    tap_aps = {}
    for tname, shape, dt in taps:
        tap_aps[tname] = nc.dram_tensor(tname, shape, dt, kind="ExternalOutput").ap()

    guards = {}

    def sb(name, shape, dtype):
        g = nc.sbuf_tensor(name, shape, dtype)
        t = g.__enter__()
        guards[name] = g
        return t

    def sbfree(*names):
        for name in names:
            guards.pop(name).__exit__(None, None, None)

    def ps(name, shape):
        g = nc.psum_tensor(name, shape, F32)
        t = g.__enter__()
        guards[name] = g
        return t

    def fms(tens, t0, nt, c, p0=0, np_=128):
        return tens[p0:p0 + np_, t0:t0 + nt, c, :]

    def r3(ap2d, b=128):
        return ap2d.rearrange("p (a b) -> p a b", b=b)

    ident_s = sb("ident_s", [128, 128], F32)
    maskT_s = sb("maskT_s", [128, 128], F32)
    mask01_s = sb("mask01_s", [128, 128], BF16)
    onesc_s = sb("onesc_s", [128, 1], BF16)
    onesrb_s = sb("onesrb_s", [1, 128], BF16)
    onesrf_s = sb("onesrf_s", [1, 128], F32)
    onesr64_s = sb("onesr64_s", [1, 64], BF16)
    bqkv_s = sb("bqkv_s", [128, 24], F32)
    bksc_s = sb("bksc_s", [128, 8], F32)
    bout_s = sb("bout_s", [128, 8], F32)
    cnb1_s = sb("cnb1_s", [128, 8], F32)
    cnb2_s = sb("cnb2_s", [128, 4], F32)
    cnb3_s = sb("cnb3_s", [128, 2], F32)
    cnw4_s = sb("cnw4_s", [128, 2], BF16)
    gcb1_s = sb("gcb1_s", [128, 4], F32)
    gcb2_s = sb("gcb2_s", [128, 2], F32)
    gcw3_s = sb("gcw3_s", [128, 2], BF16)
    scal_s = sb("scal_s", [1, 16], F32)
    gcr_s = sb("gcr_s", [128, 8], BF16)
    gcp_s = sb("gcp_s", [128, 32], F32)
    gch1_s = sb("gch1_s", [128, 4], BF16)
    gch2_s = sb("gch2_s", [128, 2], BF16)
    rows_s = sb("rows_s", [1, 4096], F32)
    aux_s = sb("aux_s", [1, 1024], F32)
    rowb_s = sb("rowb_s", [1, 1024], BF16)
    lnt_s = sb("lnt_s", [128, 2048], F32)
    wscr_s = sb("wscr_s", [128, 2048], F32)
    wscr2_s = sb("wscr2_s", [128, 2048], F32)
    xb0_s = sb("xb0_s", [128, 512], BF16)
    xb1_s = sb("xb1_s", [128, 512], BF16)
    sq0_s = sb("sq0_s", [128, 512], BF16)
    sq1_s = sb("sq1_s", [128, 512], BF16)
    big_s = sb("big_s", [128, NT, NCH, 128], BF16)

    pA = ps("pA", [128, 512])
    pB = ps("pB", [128, 512])
    pO0 = ps("pO0", [128, 512])
    pO1 = ps("pO1", [128, 512])
    pR0 = ps("pR0", [128, 512])
    pR1 = ps("pR1", [128, 512])
    pRowA = ps("pRowA", [128, 512])
    pRowB = ps("pRowB", [128, 512])

    kb.begin()
    for dst, src in ((ident_s, cst["ident"]), (maskT_s, cst["maskT"]),
                     (mask01_s, cst["maskT01"]),
                     (onesc_s, cst["ones_c128"]), (onesrb_s, cst["ones_r128_bf"]),
                     (onesrf_s, cst["ones_r128_f"]), (onesr64_s, cst["ones_r64"]),
                     (bqkv_s, din["b_qkv"]), (bout_s, din["b_out_c"]),
                     (cnb1_s, din["cn_b1c"]), (cnb2_s, din["cn_b2c"]),
                     (cnb3_s, din["cn_b3c"]), (gcb1_s, din["gc_b1c"]),
                     (gcb2_s, din["gc_b2c"])):
        kb.emit("sp", lambda e, d=dst, s=src: e.dma_start(d[:], s[:]),
                inc="dC", inc_n=16)
    kb.emit("sp", lambda e: e.dma_start(scal_s[0:1, 8:9], din["cn_b4"][:]), inc="dC", inc_n=16)
    kb.emit("sp", lambda e: e.dma_start(scal_s[0:1, 9:10], din["gc_b3"][:]), inc="dC", inc_n=16)
    kb.emit("sp", lambda e: e.dma_start(scal_s[0:1, 2:3], din["adrange"][:]), inc="dC", inc_n=16)
    kb.emit("sp", lambda e: e.dma_start(wscr_s[:, 0:2], din["cn_w4c"][:]), inc="dC", inc_n=16)
    kb.emit("sp", lambda e: e.dma_start(wscr_s[:, 2:4], din["gc_w3c"][:]), inc="dC", inc_n=16)
    tall = kb.tok("dC")
    kb.emit("dve", lambda e: e.tensor_copy(cnw4_s[:], wscr_s[:, 0:2]), waits=[tall])
    kb.emit("dve", lambda e: e.tensor_copy(gcw3_s[:], wscr_s[:, 2:4]), waits=[tall])
    kb.emit("act", lambda e: e.mul(bksc_s[:], bqkv_s[:, 8:16], 0.125), waits=[tall])
    kb.emit("act", lambda e: e.mul(scal_s[0:1, 0:1], scal_s[0:1, 8:9], 0.5), waits=[tall])
    kb.emit("act", lambda e: e.mul(scal_s[0:1, 1:2], scal_s[0:1, 9:10], 0.5), waits=[tall])
    kb.run("ph0")

    wqb = (sb("wqb0", [128, 1024], BF16), sb("wqb1", [128, 1024], BF16))
    wkb = (sb("wkb0", [128, 1024], BF16), sb("wkb1", [128, 1024], BF16))
    wvb = (sb("wvb0", [128, 1024], BF16), sb("wvb1", [128, 1024], BF16))
    wq_ready = [None, None]
    wchain_state = [None, 0, None]

    def emit_wchain(cp):
        pp = cp % 2
        ct = None
        for wi, (wb, coff) in enumerate(((wqb, 0), (wkb, D), (wvb, 2 * D))):
            gi = wchain_state[1]
            wchain_state[1] += 1
            scr = (wscr_s, wscr2_s)[gi % 2]
            col = coff + cp * 128
            td = kb.emit("sp", lambda e, col=col, scr=scr: e.dma_start(
                scr[:, 0:1024].rearrange("p (k j) -> p k j", j=128),
                din["w_in"][:, col:col + 128].rearrange("(k p) j -> p k j", p=128)),
                waits=[wchain_state[0] if gi % 2 == 0 else wchain_state[2]],
                inc="dWin%d" % (gi % 2), inc_n=16)
            ct = kb.emit("po", lambda e, wb=wb, pp=pp, scr=scr: e.tensor_copy(
                wb[pp][:], scr[:, 0:1024]), waits=[td, w_free[pp]], inc="po")
            if gi % 2 == 0:
                wchain_state[0] = ct
            else:
                wchain_state[2] = ct
        wq_ready[pp] = ct

    w_free = [None, None]
    ln1T = sb("ln1T", [128, NT, NCH, 128], BF16)
    xT = sb("xT", [128, NT, NCH, 128], F32)
    land = (sb("land0", [128, 1024], F32), sb("land1", [128, 1024], F32))

    kb.begin()
    land_free = [None, None]
    tile_done = {}
    for t in range(NT):
        lb = land[t % 2]
        pp = ((pA, pB), (pO0, pO1))[t % 2]
        tdma = kb.emit("sp", lambda e, lb=lb, t=t: e.dma_start(
            lb[:], din["x"][t * 128:(t + 1) * 128, :]),
            waits=[land_free[t % 2]], inc="dL%d" % (t % 2), inc_n=16)
        for half in range(2):
            pt = pp[half]
            tp = None
            for cc in range(4):
                c = half * 4 + cc
                tp = kb.emit("pe", lambda e, pt=pt, cc=cc, lb=lb, c=c: e.transpose(
                    pt[:, cc * 128:(cc + 1) * 128], lb[:, c * 128:(c + 1) * 128],
                    ident_s[:]), waits=[tdma, tile_done.get(t - 2)], inc="pe")
            tv = kb.emit("dve", lambda e, pt=pt, t=t, half=half: e.tensor_copy(
                xT[:, t, half * 4:half * 4 + 4, :], r3(pt[:])),
                waits=[tp], inc="dve")
        tile_done[t] = tv
        land_free[t % 2] = tp
    emit_wchain(0)
    emit_wchain(1)
    spill_last = None
    ln1_done = {}
    rowA_free = rowB_free = None
    for u in range(NU):
        tx = tile_done[4 * u + 3]
        mm_s = mm_q = None
        for c in range(NCH):
            xb = (xb0_s, xb1_s)[c % 2]
            sq = (sq0_s, sq1_s)[c % 2]
            tc = kb.emit("act", lambda e, xb=xb, u=u, c=c: e.activation(
                r3(xb[:]), fms(xT, 4 * u, 4, c), AF.Copy), waits=[tx, mm_s], inc="act")
            tq = kb.emit("po", lambda e, xb=xb, sq=sq: e.tensor_tensor(
                sq[:], xb[:], xb[:], OP.mult), waits=[tc], inc="po")
            mm_s = kb.emit("pe", lambda e, xb=xb, c=c: e.matmul(
                pRowA[0:1, :], onesc_s[:], xb[:], start=(c == 0), stop=(c == NCH - 1)),
                waits=[tc, rowA_free], inc="pe")
            mm_q = kb.emit("pe", lambda e, sq=sq, c=c: e.matmul(
                pRowB[0:1, :], onesc_s[:], sq[:], start=(c == 0), stop=(c == NCH - 1)),
                waits=[tq, rowB_free], inc="pe")
        m_bf, r_bf = rowb_s[0:1, 0:512], rowb_s[0:1, 512:1024]
        a0, a1 = aux_s[0:1, 0:512], aux_s[0:1, 512:1024]
        t1 = kb.emit("dve", lambda e, m_bf=m_bf: e.tensor_scalar(
            m_bf[:], pRowA[0:1, :], 1.0 / D, None, OP.mult), waits=[mm_s], inc="dve")
        t2 = kb.emit("dve", lambda e, a0=a0, m_bf=m_bf: e.tensor_tensor(
            a0[:], m_bf[:], m_bf[:], OP.mult), waits=[t1], inc="dve")
        t3 = kb.emit("dve", lambda e, a1=a1: e.tensor_scalar(
            a1[:], pRowB[0:1, :], 1.0 / D, LN_EPS, OP.mult, OP.add), waits=[mm_q], inc="dve")
        t4 = kb.emit("dve", lambda e, a0=a0, a1=a1: e.tensor_tensor(
            a0[:], a1[:], a0[:], OP.subtract), waits=[t2, t3], inc="dve")
        rowA_free, rowB_free = t1, t3
        t5 = kb.emit("act", lambda e, a0=a0, a1=a1: e.activation(
            a1[:], a0[:], AF.Ln), waits=[t4], inc="act")
        t6 = kb.emit("act", lambda e, a1=a1, r_bf=r_bf: e.activation(
            r_bf[:], a1[:], AF.Exp, scale=-0.5), waits=[t5], inc="act")
        bm = kb.emit("pe", lambda e, m_bf=m_bf: e.matmul(
            pR0[:], onesrb_s[:], m_bf[:], start=True, stop=True),
            waits=[t1, ln1_done.get(u - 1)], inc="pe")
        br = kb.emit("pe", lambda e, r_bf=r_bf: e.matmul(
            pR1[:], onesrb_s[:], r_bf[:], start=True, stop=True),
            waits=[t6], inc="pe")
        tn = None
        for ch in range(2):
            c0 = ch * 4
            lv = lnt_s[:].rearrange("p (a b j) -> p a b j", b=4, j=128)
            bc0 = r3(pR0[:]).unsqueeze(2).broadcast_to([128, 4, 4, 128])
            bc1 = r3(pR1[:]).unsqueeze(2).broadcast_to([128, 4, 4, 128])
            tt1 = kb.emit("dve", lambda e, u=u, c0=c0, lv=lv, bc0=bc0: e.tensor_tensor(
                lv, xT[:, 4 * u:4 * u + 4, c0:c0 + 4, :], bc0, OP.subtract),
                waits=[bm, br], inc="dve")
            tn = kb.emit("dve", lambda e, u=u, c0=c0, lv=lv, bc1=bc1: e.tensor_tensor(
                ln1T[:, 4 * u:4 * u + 4, c0:c0 + 4, :], lv, bc1, OP.mult),
                waits=[tt1], inc="dve")
        ln1_done[u] = tn
        spill_last = kb.emit("sp", lambda e, u=u: e.dma_start(
            xspill[:, u * 4096:(u + 1) * 4096].rearrange(
                "p (a b c) -> p a b c", b=NCH, c=128),
            xT[:, 4 * u:4 * u + 4, :, :]), waits=[tn], inc="dSp", inc_n=16)
    kb.wait_only("sp", [kb.tok("dSp")])
    kb.run("ph1")
    if phases <= 1:
        return nc

    if "tap_ln1T" in tap_aps:
        kb.begin()
        kb.emit("po", lambda e: e.dma_start(
            tap_aps["tap_ln1T"][:].rearrange("p (t c j) -> p t c j", c=NCH, j=128),
            ln1T[:, :, :, :]), inc="dT", inc_n=16)
        kb.wait_only("po", [kb.tok("dT")])
        kb.run("ph1t")

    sbfree("land1", "land0", "xT")

    qh = ((sb("qh00", [128, L], BF16), sb("qh01", [128, L], BF16)),
          (sb("qh10", [128, L], BF16), sb("qh11", [128, L], BF16)))
    kh = ((sb("kh00", [128, L], BF16), sb("kh01", [128, L], BF16)),
          (sb("kh10", [128, L], BF16), sb("kh11", [128, L], BF16)))
    vfm = (sb("vfm0", [128, L], BF16), sb("vfm1", [128, L], BF16))
    vt = ((sb("vt00", [128, NT, 128], BF16), sb("vt01", [128, NT, 128], BF16)),
          (sb("vt10", [128, NT, 128], BF16), sb("vt11", [128, NT, 128], BF16)))
    PT = tuple(sb("pt%d" % i, [128, 512], BF16) for i in range(5))
    rsb = sb("rsb", [128, 512], BF16)
    pQK = (pR1,)

    kb.begin()
    spill_done = kb.tok("dSp")
    kb.wait_only("pe", [ln1_done[NU - 1]])
    init_dve = None
    for pr in qh:
        for t_ in pr:
            init_dve = kb.emit("dve", lambda e, t_=t_: e.memset(t_[64:67, :], 1.0),
                               waits=[spill_done], inc="dve")
    for pr in vt:
        init_dve = kb.emit("dve", lambda e, pr=pr: e.memset(pr[0][:, :, 64:65], 1.0), inc="dve")
        init_dve = kb.emit("dve", lambda e, pr=pr: e.memset(pr[1][:, :, 0:1], 1.0), inc="dve")
        init_dve = kb.emit("dve", lambda e, pr=pr: e.memset(pr[1][:, :, 1:64], 0.0), inc="dve")
    init_dg = [[None, None], [None, None]]
    for pp in range(2):
        for h01 in range(2):
            init_dg[pp][h01] = kb.emit("po", lambda e, t_=kh[pp][h01]: e.dma_start(
                t_[67:68, :], cst["minus1"][:]), waits=[spill_done],
                inc="dK%d%d" % (pp, h01), inc_n=16)

    q_free = [[None, None], [None, None]]
    v_free = [[None, None], [None, None]]
    vfm_free = [[None] * NU, [None] * NU]
    vtr_tok = {}
    pt_free = [None] * 5
    pS_free = [None, None, None]
    gkt = [0]
    pO_free = [None, None]
    pQK_free = [None, None]
    pR0_free = [None]
    rsb_free = [None]
    r_free = [None, None]
    escr_free = [None, None]
    evq = [[None], [None]]
    evk = [[None], [None]]
    evv = [[None], [None]]
    head_tok_p = [None, None]
    wscr_free = None
    flush_qe = [None]
    pending_evac = None
    pending_pvs = []
    qkv_gi = [0]

    def qkv_filler(cp):
        pp = cp % 2
        steps = []
        for mi, wb in enumerate((wqb, wkb, wvb)):
            for u in range(NU):
                gi = qkv_gi[0]
                qkv_gi[0] += 1
                last = {}
                for kc in range(NCH):
                    def mm_step(kc=kc, gi=gi, wb=wb, pp=pp, u=u, last=last):
                        pg = pQK[0]
                        last["mm"] = kb.emit("pe", lambda e, pg=pg, wb=wb, pp=pp, kc=kc, u=u: e.matmul(
                            pg[:], wb[pp][:, kc * 128:(kc + 1) * 128],
                            fms(ln1T, 4 * u, 4, kc),
                            start=(kc == 0), stop=(kc == NCH - 1)),
                            waits=[wq_ready[pp], pQK_free[0]], inc="pe")
                    steps.append(mm_step)
                def evac_step(mi=mi, u=u, gi=gi, pp=pp, cp=cp, last=last):
                    pg = pQK[0]
                    mm = last["mm"]
                    if mi < 2:
                        dst = qh[pp] if mi == 0 else kh[pp]
                        scr = (xb0_s, xb1_s)[gi % 2]
                        if mi == 0:
                            tv = kb.emit("dve", lambda e, pg=pg, scr=scr, cp=cp: e.tensor_scalar(
                                scr[:], pg[:], bqkv_s[:, cp:cp + 1], None, OP.add),
                                waits=[mm, escr_free[gi % 2]], inc="dve")
                        else:
                            tv = kb.emit("dve", lambda e, pg=pg, scr=scr, cp=cp: e.tensor_scalar(
                                scr[:], pg[:], 0.125, bksc_s[:, cp:cp + 1],
                                OP.mult, OP.add),
                                waits=[mm, escr_free[gi % 2]], inc="dve")
                        tsh = None
                        for h01 in range(2):
                            tsh = kb.emit("po", lambda e, scr=scr, h01=h01, dst=dst, u=u: e.dma_start(
                                dst[h01][0:64, u * 512:(u + 1) * 512],
                                scr[h01 * 64:h01 * 64 + 64, :]),
                                waits=[tv, q_free[pp][h01]],
                                inc="dE%d" % (gi % 2), inc_n=16)
                        escr_free[gi % 2] = tsh
                        (evq if mi == 0 else evk)[pp][0] = tsh
                        pQK_free[0] = tv
                    else:
                        tv = kb.emit("dve", lambda e, pg=pg, pp=pp, u=u, cp=cp: e.tensor_scalar(
                            vfm[pp][:, u * 512:(u + 1) * 512], pg[:],
                            bqkv_s[:, 16 + cp:17 + cp], None, OP.add),
                            waits=[mm, vfm_free[pp][u]], inc="dve")
                        evv[pp][0] = tv
                        pQK_free[0] = tv
                        last_t = None
                        for h01 in range(2):
                            sem = "dV%d%d%d" % (pp, h01, u)
                            for tti in range(4):
                                ti = 4 * u + tti
                                c0 = 0 if h01 == 0 else 64
                                qe = flush_qe[0] or ("sp", "act")[tti % 2]
                                last_t = kb.emit(qe, lambda e, h01=h01, ti=ti, c0=c0, pp=pp: e.dma_start_transpose(
                                    vt[pp][h01][:, ti, c0:c0 + 64],
                                    vfm[pp][h01 * 64:h01 * 64 + 64, ti * 128:(ti + 1) * 128]),
                                    waits=[tv, v_free[pp][h01]],
                                    inc=sem, inc_n=16)
                            vtr_tok[(pp, h01, u)] = last_t
                        vfm_free[pp][u] = last_t
                steps.append(evac_step)
        return steps

    def bprep(cp):
        pp = cp % 2
        toks = []
        for h01 in range(2):
            h = 2 * cp + h01
            tr3 = kb.emit("po", lambda e, h=h, h01=h01, pp=pp: e.dma_start(
                kh[pp][h01][64:67, :], cst["ramp3"][3 * h:3 * h + 3, :]),
                waits=[q_free[pp][h01], evk[pp][0]], inc="dK%d%d" % (pp, h01), inc_n=16)
            tbp = kb.emit("po", lambda e, h=h, h01=h01, pp=pp: e.dma_start(
                qh[pp][h01][67:68, :], cst["ramp4"][h:h + 1, :]),
                waits=[q_free[pp][h01], evq[pp][0]], inc="dB%d%d" % (pp, h01), inc_n=16)
            toks.append((tr3, tbp))
        head_tok_p[pp] = toks

    for st in qkv_filler(0):
        st()
    bprep(0)

    for cp in range(NP):
        pp = cp % 2
        filler = qkv_filler(cp + 1) if cp + 1 < NP else []
        if cp + 2 < NP:
            emit_wchain(cp + 2)
        nblocks = 2 * NQ
        bi = 0
        for h01 in range(2):
            tr3, tbp = head_tok_p[pp][h01]
            last_mm = None
            for t in range(NQ):
                oi = (h01 * NQ + t) % 2
                pO = (pO0, pO1)[oi]
                Wh = 94.0 / alibi_slopes()[2 * cp + h01]
                kts = [4 * t + j for j in range(4)] + [
                    kt for kt in range(4 * t) if 128 * kt + 127 >= 512 * t - Wh]
                nk = len(kts)
                exps = [None] * nk
                def emit_pv(i, pO=pO, pp=pp, h01=h01, kts=None, nk=0, exps=None, t=t, oi=oi):
                    kt = kts[i]
                    j = kt - 4 * t if kt >= 4 * t else None
                    col0 = 128 * j if j is not None else 0
                    etok, pti = exps[i]
                    mrange = 65 if h01 == 0 else 128
                    tok = kb.emit("pe", lambda e, pO=pO, pti=pti, col0=col0, h01=h01, kt=kt, i=i, nk=nk, mrange=mrange, pp=pp: e.matmul(
                        pO[0:mrange, col0:512], vt[pp][h01][:, kt, 0:mrange], PT[pti][:, col0:512],
                        start=(i == 0), stop=(i == nk - 1)),
                        waits=[etok, vtr_tok[(pp, h01, kt // 4)], init_dve,
                               pO_free[oi] if i == 0 else None], inc="pe")
                    return (tok, pti)
                last_pv = None
                for i, kt in enumerate(kts):
                    si = gkt[0] % 3
                    pS = (pA, pB, pRowB)[si]
                    j = kt - 4 * t if kt >= 4 * t else None
                    col0 = 128 * j if j is not None else 0
                    mm = kb.emit("pe", lambda e, pS=pS, kt=kt, t=t, col0=col0, h01=h01, pp=pp: e.matmul(
                        pS[:, col0:512], kh[pp][h01][0:68, kt * 128:(kt + 1) * 128],
                        qh[pp][h01][0:68, t * 512 + col0:(t + 1) * 512],
                        start=True, stop=True),
                        waits=[tr3, tbp, init_dg[pp][h01], pS_free[si]],
                        inc="pe")
                    last_mm = mm
                    if i in (1, 2, 3) and pending_pvs:
                        pending_pvs.pop(0)()
                    if i == 4 and pending_evac is not None:
                        pending_evac()
                        pending_evac = None
                    pti = gkt[0] % 5
                    gkt[0] += 1
                    texp = kb.emit("act", lambda e, pS=pS, pti=pti, col0=col0: e.activation(
                        PT[pti][:, col0:512], pS[:, col0:512], AF.Exp),
                        waits=[mm, pt_free[pti]], inc="act")
                    if j is not None:
                        texp = kb.emit("dve", lambda e, pti=pti, col0=col0: e.tensor_tensor(
                            PT[pti][:, col0:col0 + 128], PT[pti][:, col0:col0 + 128],
                            mask01_s[:], OP.mult), waits=[texp], inc="dve")
                    exps[i] = (texp, pti)
                    pS_free[si] = texp
                    if i >= 2:
                        last_pv, lpti = emit_pv(i - 2, kts=kts, nk=nk, exps=exps)
                        pt_free[lpti] = last_pv
                    if filler:
                        filler.pop(0)()
                if pending_evac is not None:
                    pending_evac()
                    pending_evac = None
                tail_state = {}
                def mk_tail(ii, kts=kts, nk=nk, exps=exps, pp=pp, h01=h01, tail_state=tail_state, emit_pv=emit_pv):
                    def go():
                        tok, lpti = emit_pv(ii, kts=kts, nk=nk, exps=exps)
                        pt_free[lpti] = tok
                        tail_state["last"] = tok
                        if ii == nk - 1:
                            v_free[pp][h01] = tok
                    return go
                for ii in (nk - 2, nk - 1):
                    pending_pvs.append(mk_tail(ii))
                bi += 1
                while len(filler) > 30 + 12 * (nblocks - bi):
                    filler.pop(0)()
                srow = 64 if h01 == 0 else 0
                a_r = aux_s[0:1, 512:1024]
                r_bf = rowb_s[0:1, oi * 512:(oi + 1) * 512]
                def mk_parta(pO=pO, oi=oi, srow=srow, a_r=a_r, r_bf=r_bf,
                             tail_state=tail_state):
                    def go():
                        lpv = tail_state["last"]
                        tl = kb.emit("act", lambda e, pO=pO, srow=srow, a_r=a_r: e.activation(
                            a_r[:], pO[srow:srow + 1, 0:512], AF.Ln),
                            waits=[lpv, r_free[oi]], inc="act")
                        def _rexp(e, a_r=a_r, r_bf=r_bf):
                            with nc.allow_low_precision(reason="1/s as bf16 operand"):
                                return e.activation(r_bf[:], a_r[:], AF.Exp, scale=-1.0)
                        tail_state["tre"] = kb.emit("act", _rexp, waits=[tl], inc="act")
                    return go
                pending_pvs.append(mk_parta())
                def mk_evac(pO=pO, oi=oi, h01=h01, t=t, cp=cp, r_bf=r_bf,
                            tail_state=tail_state):
                    def go():
                        tre = tail_state["tre"]
                        tRm = kb.emit("pe", lambda e, r_bf=r_bf: e.matmul(
                            pR0[:], onesrb_s[:], r_bf[:], start=True, stop=True),
                            waits=[tre, pR0_free[0]], inc="pe")
                        r_free[oi] = tRm
                        tRe = kb.emit("dve", lambda e: e.tensor_copy(
                            rsb[:], pR0[:]), waits=[tRm, rsb_free[0]], inc="dve")
                        pR0_free[0] = tRe
                        p0 = h01 * 64
                        tev = kb.emit("dve", lambda e, pO=pO, t=t, cp=cp, p0=p0: e.tensor_tensor(
                            fms(big_s, 4 * t, 4, cp, p0, 64),
                            r3(pO[p0:p0 + 64, 0:512]), r3(rsb[p0:p0 + 64, :]), OP.mult),
                            waits=[tRe], inc="dve")
                        pO_free[oi] = tev
                        rsb_free[0] = tev
                    return go
                pending_evac = mk_evac()
            q_free[pp][h01] = last_mm
        for st in pending_pvs:
            st()
        pending_pvs = []
        flush_qe[0] = "sp"
        for st in filler:
            st()
        flush_qe[0] = None
        if cp + 1 < NP:
            bprep(cp + 1)
    if pending_evac is not None:
        pending_evac()
        pending_evac = None
    kb.run("ph2")
    if phases <= 2:
        return nc

    if "tap_aout" in tap_aps:
        kb.begin()
        kb.emit("po", lambda e: e.dma_start(
            tap_aps["tap_aout"][:].rearrange("p (t c j) -> p t c j", c=NCH, j=128),
            big_s[:, :, :, :]), inc="dT", inc_n=16)
        kb.wait_only("po", [kb.tok("dT")])
        kb.run("ph2t")

    sbfree("rsb", "pt4", "pt3", "pt2", "pt1", "pt0",
           "vt11", "vt10", "vt01", "vt00", "vfm1", "vfm0",
           "kh11", "kh10", "kh01", "kh00", "qh11", "qh10", "qh01", "qh00",
           "ln1T", "wvb1", "wvb0", "wkb1", "wkb0", "wqb1", "wqb0")

    xT2 = sb("xT2", [128, NT, NCH, 128], F32)
    cn1b = sb("cn1b", [128, 8, 1024], BF16)
    cn2b = sb("cn2b", [128, 8, 512], BF16)
    cn3b = sb("cn3b", [128, 4, 256], BF16)
    gc1b = sb("gc1b", [128, 8, 512], BF16)
    gc2b = sb("gc2b", [128, 4, 256], BF16)
    woutb = sb("woutb", [128, 8, 1024], BF16)

    kb.begin()
    chains = [(woutb, din["w_out"], NCH, 1024), (cn1b, din["cn_w1"], 8, 1024),
              (gc1b, din["gc_w1"], 8, 512), (cn2b, din["cn_w2"], 8, 512),
              (gc2b, din["gc_w2"], 4, 256), (cn3b, din["cn_w3"], 4, 256)]
    wq_casts = {}
    gidx = 0
    cast_free = [None, None]
    for dst, wsrc, nkc, ncol in chains:
        for kc0 in range(0, nkc, 2):
            nk2 = min(2, nkc - kc0)
            scr = (wscr_s, wscr2_s)[gidx % 2]
            qe = ("sp", "po")[gidx % 2]
            td = kb.emit(qe, lambda e, wsrc=wsrc, kc0=kc0, nk2=nk2, ncol=ncol, scr=scr: e.dma_start(
                scr[:, 0:nk2 * ncol].rearrange("p (a b) -> p a b", b=ncol),
                wsrc[kc0 * 128:(kc0 + nk2) * 128, :].rearrange(
                    "(a p) b -> p a b", p=128)),
                waits=[cast_free[gidx % 2]], inc="dW%d" % (gidx % 2), inc_n=16)
            ct = kb.emit("act", lambda e, dst=dst, kc0=kc0, nk2=nk2, ncol=ncol, scr=scr: e.activation(
                dst[:, kc0:kc0 + nk2, 0:ncol],
                scr[:, 0:nk2 * ncol].rearrange("p (a b) -> p a b", b=ncol),
                AF.Copy), waits=[td], inc="act")
            cast_free[gidx % 2] = ct
            gidx += 1
        wq_casts[dst.name] = ct
    wcast = wq_casts["woutb"]
    ffnw_tok = ct
    xload = []
    for u in range(NU):
        xload.append(kb.emit("sp", lambda e, u=u: e.dma_start(
            xT2[:, 4 * u:4 * u + 4, :, :],
            xspill[:, u * 4096:(u + 1) * 4096].rearrange(
                "p (a b c) -> p a b c", b=NCH, c=128)), inc="dX", inc_n=16))
    pY_free = [None, None, None, None]
    tev_tok = {}
    for m in range(NCH):
        for u in range(NU):
            yi = (m * NU + u) % 4
            pY = (pA, pB, pO0, pO1)[yi]
            mm = None
            for kc in range(NCH):
                mm = kb.emit("pe", lambda e, pY=pY, kc=kc, m=m, u=u: e.matmul(
                    pY[:], woutb[:, kc, m * 128:(m + 1) * 128],
                    fms(big_s, 4 * u, 4, kc),
                    start=(kc == 0), stop=(kc == NCH - 1)),
                    waits=[wcast, pY_free[yi]], inc="pe")
            tev = kb.emit("dve", lambda e, pY=pY, m=m, u=u: e.scalar_tensor_tensor(
                fms(xT2, 4 * u, 4, m), r3(pY[:]), bout_s[:, m:m + 1],
                fms(xT2, 4 * u, 4, m), OP.add, OP.add),
                waits=[mm, xload[-1]], inc="dve")
            tev_tok[(m, u)] = tev
            pY_free[yi] = tev
    kb.run("ph3")
    if phases <= 3:
        return nc

    if "tap_x2" in tap_aps:
        kb.begin()
        kb.emit("po", lambda e: e.dma_start(
            tap_aps["tap_x2"][:].rearrange("p (t c j) -> p t c j", c=NCH, j=128),
            xT2[:, :, :, :]), inc="dT", inc_n=16)
        kb.wait_only("po", [kb.tok("dT")])
        kb.run("ph3t")

    sbfree("woutb")

    outT = sb("outT", [128, 8, 256], F32)
    h1b = sb("h1b", [128, 8, 256], BF16)
    h2b = sb("h2b", [128, 4, 256], BF16)
    h3b = sb("h3b", [128, 2, 256], BF16)
    osc = (wscr_s[:, 0:1024], wscr2_s[:, 0:1024])

    kb.begin()
    rowA_free = rowB_free = None
    ln2_done = {}
    ln2_rows = {}
    gc_last = None
    for u in range(NU):
        mm_s = mm_q = None
        for c in range(NCH):
            xb = (xb0_s, xb1_s)[c % 2]
            sq = (sq0_s, sq1_s)[c % 2]
            tc = kb.emit("act", lambda e, xb=xb, u=u, c=c: e.activation(
                r3(xb[:]), fms(xT2, 4 * u, 4, c), AF.Copy),
                waits=[mm_s, tev_tok[(c, u)]], inc="act")
            tq = kb.emit("po", lambda e, xb=xb, sq=sq: e.tensor_tensor(
                sq[:], xb[:], xb[:], OP.mult), waits=[tc], inc="po")
            mm_s = kb.emit("pe", lambda e, xb=xb, c=c: e.matmul(
                pRowA[0:1, :], onesc_s[:], xb[:], start=(c == 0), stop=(c == NCH - 1)),
                waits=[tc, rowA_free], inc="pe")
            mm_q = kb.emit("pe", lambda e, sq=sq, c=c: e.matmul(
                pRowB[0:1, :], onesc_s[:], sq[:], start=(c == 0), stop=(c == NCH - 1)),
                waits=[tq, rowB_free], inc="pe")
        m_f = rows_s[0:1, u * 512:(u + 1) * 512]
        r_f = rows_s[0:1, 2048 + u * 512:2048 + (u + 1) * 512]
        a0, a1 = aux_s[0:1, 0:512], aux_s[0:1, 512:1024]
        t1 = kb.emit("dve", lambda e, m_f=m_f: e.tensor_scalar(
            m_f[:], pRowA[0:1, :], 1.0 / D, None, OP.mult), waits=[mm_s], inc="dve")
        t2 = kb.emit("dve", lambda e, a0=a0, m_f=m_f: e.tensor_tensor(
            a0[:], m_f[:], m_f[:], OP.mult), waits=[t1], inc="dve")
        t3 = kb.emit("dve", lambda e, a1=a1: e.tensor_scalar(
            a1[:], pRowB[0:1, :], 1.0 / D, LN_EPS, OP.mult, OP.add), waits=[mm_q], inc="dve")
        t4 = kb.emit("dve", lambda e, a0=a0, a1=a1: e.tensor_tensor(
            a0[:], a1[:], a0[:], OP.subtract), waits=[t2, t3], inc="dve")
        rowA_free, rowB_free = t1, t3
        t5 = kb.emit("act", lambda e, a0=a0, a1=a1: e.activation(
            a1[:], a0[:], AF.Ln), waits=[t4], inc="act")
        t6 = kb.emit("act", lambda e, a1=a1, r_f=r_f: e.activation(
            r_f[:], a1[:], AF.Exp, scale=-0.5), waits=[t5], inc="act")
        ln2_rows[u] = (t1, t6)
        bm = kb.emit("pe", lambda e, m_f=m_f: e.matmul(
            pR0[:], onesrf_s[:], m_f[:], start=True, stop=True),
            waits=[t1, ln2_done.get(u - 1)], inc="pe")
        br = kb.emit("pe", lambda e, r_f=r_f: e.matmul(
            pR1[:], onesrf_s[:], r_f[:], start=True, stop=True),
            waits=[t6], inc="pe")
        tn = None
        for ch in range(2):
            c0 = ch * 4
            lv = lnt_s[:].rearrange("p (a b j) -> p a b j", b=4, j=128)
            bc0 = r3(pR0[:]).unsqueeze(2).broadcast_to([128, 4, 4, 128])
            bc1 = r3(pR1[:]).unsqueeze(2).broadcast_to([128, 4, 4, 128])
            tt1 = kb.emit("dve", lambda e, u=u, c0=c0, lv=lv, bc0=bc0: e.tensor_tensor(
                lv, xT2[:, 4 * u:4 * u + 4, c0:c0 + 4, :], bc0, OP.subtract),
                waits=[bm, br], inc="dve")
            tn = kb.emit("dve", lambda e, u=u, c0=c0, lv=lv, bc1=bc1: e.tensor_tensor(
                big_s[:, 4 * u:4 * u + 4, c0:c0 + 4, :], lv, bc1, OP.mult),
                waits=[tt1], inc="dve")
        gc_last = kb.emit("dve", lambda e, u=u: e.tensor_reduce(
            gcp_s[:, u * 8:(u + 1) * 8],
            big_s[:, 4 * u:4 * u + 4, :, :].transpose([0, 2, 1, 3]),
            AX.XY, OP.add), waits=[tn], inc="dve")
        ln2_done[u] = tn
    tgs = gc_last
    for u in range(1, NU):
        tgs = kb.emit("dve", lambda e, u=u: e.tensor_tensor(
            gcp_s[:, 0:8], gcp_s[:, 0:8], gcp_s[:, u * 8:u * 8 + 8], OP.add),
            waits=[tgs], inc="dve")
    tgr = kb.emit("dve", lambda e: e.tensor_scalar(
        gcr_s[:], gcp_s[:, 0:8], 1.0 / L, None, OP.mult), waits=[tgs], inc="dve")
    gcw_tok = wq_casts["gc2b"]
    mms = []
    for m in range(4):
        mm = None
        for kc in range(NCH):
            mm = kb.emit("pe", lambda e, m=m, kc=kc: e.matmul(
                pO0[:, m:m + 1], gc1b[:, kc, m * 128:(m + 1) * 128],
                gcr_s[:, kc:kc + 1], start=(kc == 0), stop=(kc == NCH - 1)),
                waits=[tgr, gcw_tok], inc="pe")
        mms.append(mm)
    g1 = []
    for m in range(4):
        g1.append(kb.emit("act", lambda e, m=m: e.activation(
            gch1_s[:, m:m + 1], pO0[:, m:m + 1], AF.Gelu,
            bias=gcb1_s[:, m:m + 1]), waits=[mms[-1]], inc="act"))
    mms = []
    for m in range(2):
        mm = None
        for kc in range(4):
            mm = kb.emit("pe", lambda e, m=m, kc=kc: e.matmul(
                pO1[:, m:m + 1], gc2b[:, kc, m * 128:(m + 1) * 128],
                gch1_s[:, kc:kc + 1], start=(kc == 0), stop=(kc == 3)),
                waits=[g1[-1]], inc="pe")
        mms.append(mm)
    g2 = []
    for m in range(2):
        g2.append(kb.emit("act", lambda e, m=m: e.activation(
            gch2_s[:, m:m + 1], pO1[:, m:m + 1], AF.Gelu,
            bias=gcb2_s[:, m:m + 1]), waits=[mms[-1]], inc="act"))
    gc_pO_tok = (g1[-1], g2[-1])
    mm = None
    for kc in range(2):
        mm = kb.emit("pe", lambda e, kc=kc: e.matmul(
            pRowA[0:1, 0:1], gcw3_s[:, kc:kc + 1], gch2_s[:, kc:kc + 1],
            start=(kc == 0), stop=(kc == 1)), waits=[g2[-1]], inc="pe")
    tth_gc = kb.emit("act", lambda e: e.activation(
        scal_s[0:1, 3:4], pRowA[0:1, 0:1], AF.Tanh, scale=0.5,
        bias=scal_s[0:1, 1:2]), waits=[mm], inc="act")
    tt = kb.emit("dve", lambda e: e.tensor_tensor(
        scal_s[0:1, 7:8], scal_s[0:1, 2:3], scal_s[0:1, 3:4], OP.mult),
        waits=[tth_gc], inc="dve")
    c0_tok = kb.emit("dve", lambda e: e.tensor_scalar(
        scal_s[0:1, 4:5], scal_s[0:1, 7:8], 0.7, 0.1, OP.mult, OP.add),
        waits=[tt], inc="dve")
    c1_tok = kb.emit("dve", lambda e: e.tensor_scalar(
        scal_s[0:1, 5:6], scal_s[0:1, 2:3], 0.3, None, OP.mult), inc="dve")
    outT2 = (outT, sb("outT2", [128, 8, 256], F32))
    pC_free = [None, None]
    pCP = (pR0, pR1)
    pC4 = (pRowA, pRowB)
    cp_free = [ln2_done[NU - 1], ln2_done[NU - 1]]
    exit_free = [gc_pO_tok[0], gc_pO_tok[1]]
    c4_free = [tth_gc, rowB_free]
    prev_tcp = [None, None]
    osc_free = [None, None]
    last_twists = [None, None]
    for fp in range(0, NF, 2):
        out0_tok = [None, None]
        for p in range(2):
            f = fp + p
            t0 = f * 2
            m_f = rows_s[0:1, f * 256:(f + 1) * 256]
            r_f = rows_s[0:1, 2048 + f * 256:2048 + (f + 1) * 256]
            rt = ln2_rows[f // 2]
            bm = kb.emit("pe", lambda e, m_f=m_f: e.matmul(
                pR0[:, 0:256], onesrf_s[:], m_f[:], start=True, stop=True),
                waits=[cp_free[0], out0_tok[0] if p else None, rt[0], rt[1],
                       last_twists[0], last_twists[1]], inc="pe")
            br = kb.emit("pe", lambda e, r_f=r_f: e.matmul(
                pR1[:, 0:256], onesrf_s[:], r_f[:], start=True, stop=True),
                waits=[cp_free[1]], inc="pe")
            lv = lnt_s[:].rearrange("p (c a j) -> p c a j", a=2, j=128)
            bc0 = pR0[:, 0:256].rearrange("p (a j) -> p a j", j=128).unsqueeze(
                1).broadcast_to([128, 8, 2, 128])
            bc1 = pR1[:, 0:256].rearrange("p (a j) -> p a j", j=128).unsqueeze(
                1).broadcast_to([128, 8, 2, 128])
            tt1 = kb.emit("dve", lambda e, t0=t0, lv=lv, bc0=bc0: e.tensor_tensor(
                lv, xT2[:, t0:t0 + 2, :, :].transpose([0, 2, 1, 3]), bc0,
                OP.subtract), waits=[bm, br], inc="dve")
            tn = kb.emit("dve", lambda e, p=p, lv=lv, bc1=bc1: e.tensor_tensor(
                outT2[p][:, :, :].rearrange("p c (a j) -> p c a j", j=128),
                lv, bc1, OP.mult), waits=[tt1], inc="dve")
            out0_tok[p] = tn
        twist_tok = [out0_tok[0], out0_tok[1]]
        for cyc in range(3):
            for p in range(2):
                f = fp + p
                t0 = f * 2
                oT = outT2[p]
                g1 = []
                for m in range(NCH):
                    yi = m % 2
                    pC = (pA, pB)[yi]
                    mm = None
                    for kc in range(NCH):
                        mm = kb.emit("pe", lambda e, pC=pC, m=m, kc=kc, t0=t0: e.matmul(
                            pC[:, 0:256], cn1b[:, kc, m * 128:(m + 1) * 128],
                            fms(big_s, t0, 2, kc),
                            start=(kc == 0), stop=(kc == NCH - 1)),
                            waits=[ffnw_tok, pC_free[yi], twist_tok[p],
                                   ln2_done[t0 // 4] if cyc == 0 else None], inc="pe")
                    g1.append(kb.emit("act", lambda e, pC=pC, m=m: e.activation(
                        h1b[:, m, :], pC[:, 0:256], AF.Gelu, bias=cnb1_s[:, m:m + 1]),
                        waits=[mm], inc="act"))
                    pC_free[yi] = g1[-1]
                g2 = []
                for m in range(4):
                    yi = m % 2
                    pC = (pA, pB)[yi]
                    mm = None
                    for kc in range(NCH):
                        mm = kb.emit("pe", lambda e, pC=pC, m=m, kc=kc: e.matmul(
                            pC[:, 0:256], cn2b[:, kc, m * 128:(m + 1) * 128],
                            h1b[:, kc, :], start=(kc == 0), stop=(kc == NCH - 1)),
                            waits=[g1[kc], pC_free[yi]], inc="pe")
                    g2.append(kb.emit("act", lambda e, pC=pC, m=m: e.activation(
                        h2b[:, m, :], pC[:, 0:256], AF.Gelu, bias=cnb2_s[:, m:m + 1]),
                        waits=[mm], inc="act"))
                    pC_free[yi] = g2[-1]
                g3 = []
                for m in range(2):
                    yi = m % 2
                    pC = (pA, pB)[yi]
                    mm = None
                    for kc in range(4):
                        mm = kb.emit("pe", lambda e, pC=pC, m=m, kc=kc: e.matmul(
                            pC[:, 0:256], cn3b[:, kc, m * 128:(m + 1) * 128],
                            h2b[:, kc, :], start=(kc == 0), stop=(kc == 3)),
                            waits=[g2[kc], pC_free[yi]], inc="pe")
                    g3.append(kb.emit("act", lambda e, pC=pC, m=m: e.activation(
                        h3b[:, m, :], pC[:, 0:256], AF.Gelu, bias=cnb3_s[:, m:m + 1]),
                        waits=[mm], inc="act"))
                    pC_free[yi] = g3[-1]
                mm = None
                for kc in range(2):
                    mm = kb.emit("pe", lambda e, kc=kc, p=p: e.matmul(
                        pC4[p][0:1, 0:256], cnw4_s[:, kc:kc + 1], h3b[:, kc, :],
                        start=(kc == 0), stop=(kc == 1)),
                        waits=[g3[kc], c4_free[p]], inc="pe")
                a_th = aux_s[0:1, p * 256:(p + 1) * 256]
                a_cp = aux_s[0:1, 512 + p * 256:512 + (p + 1) * 256]
                tth = kb.emit("act", lambda e, p=p, a_th=a_th: e.activation(
                    a_th[:], pC4[p][0:1, 0:256], AF.Tanh, scale=0.5,
                    bias=scal_s[0:1, 0:1]), waits=[mm, prev_tcp[p]], inc="act")
                c4_free[p] = tth
                tcp = kb.emit("dve", lambda e, a_th=a_th, a_cp=a_cp: e.tensor_scalar(
                    a_cp[:], a_th[:], scal_s[0:1, 5:6],
                    scal_s[0:1, 4:5], OP.mult, OP.add),
                    waits=[tth, c0_tok, c1_tok], inc="dve")
                prev_tcp[p] = tcp
                tbc = kb.emit("pe", lambda e, p=p, a_cp=a_cp: e.matmul(
                    pCP[p][:, 0:256], onesrf_s[:], a_cp[:],
                    start=True, stop=True), waits=[tcp, cp_free[p]], inc="pe")
                lA = lnt_s[:, 0:1024].rearrange("p (c f) -> p c f", f=256)
                lB = lnt_s[:, 1024:2048].rearrange("p (c f) -> p c f", f=256)
                cbc = pCP[p][:, 0:256].unsqueeze(1).broadcast_to([128, 4, 256])
                tb1 = kb.emit("dve", lambda e, oT=oT, lA=lA, cbc=cbc: e.tensor_tensor(
                    lA, oT[:, 4:8, :], cbc, OP.mult), waits=[tbc], inc="dve")
                tb2 = kb.emit("dve", lambda e, oT=oT, lB=lB, cbc=cbc: e.tensor_tensor(
                    lB, oT[:, 0:4, :], cbc, OP.mult), inc="dve")
                kb.emit("dve", lambda e, oT=oT, lA=lA: e.tensor_tensor(
                    oT[:, 0:2, :], oT[:, 0:2, :], lA[:, 0:2, :], OP.add),
                    waits=[tb1], inc="dve")
                kb.emit("dve", lambda e, oT=oT, lA=lA: e.tensor_tensor(
                    oT[:, 2:4, :], oT[:, 2:4, :], lA[:, 2:4, :], OP.subtract),
                    inc="dve")
                kb.emit("dve", lambda e, oT=oT, lB=lB: e.tensor_tensor(
                    oT[:, 4:6, :], oT[:, 4:6, :], lB[:, 0:2, :], OP.subtract),
                    waits=[tb2], inc="dve")
                tw = kb.emit("dve", lambda e, oT=oT, lB=lB: e.tensor_tensor(
                    oT[:, 6:8, :], oT[:, 6:8, :], lB[:, 2:4, :], OP.add),
                    inc="dve")
                cp_free[p] = tw
                twist_tok[p] = tw
                if cyc < 2:
                    tcast = kb.emit("dve", lambda e, t0=t0, oT=oT: e.tensor_copy(
                        big_s[:, t0:t0 + 2, :, :],
                        oT[:, :, :].rearrange("p c (a j) -> p c a j", j=128
                                              ).transpose([0, 2, 1, 3])),
                        waits=[tw, g1[-1]], inc="dve")
                    twist_tok[p] = tcast
        last_twists = [twist_tok[0], twist_tok[1]]
        for p in range(2):
            f = fp + p
            t0 = f * 2
            oT = outT2[p]
            tadd = kb.emit("dve", lambda e, t0=t0, oT=oT: e.tensor_tensor(
                xT2[:, t0:t0 + 2, :, :], xT2[:, t0:t0 + 2, :, :],
                oT[:, :, :].rearrange("p c (a j) -> p c a j", j=128
                                      ).transpose([0, 2, 1, 3]), OP.add),
                waits=[twist_tok[p]], inc="dve")
            last_twists[p] = tadd
            for s in range(2):
                t = t0 + s
                ob = osc[t % 2]
                tv = None
                for half in range(2):
                    pt_ = (pO0, pO1)[half]
                    tp = None
                    for cc in range(4):
                        c = half * 4 + cc
                        tp = kb.emit("pe", lambda e, pt_=pt_, cc=cc, t=t, c=c: e.transpose(
                            pt_[:, cc * 128:(cc + 1) * 128], xT2[:, t, c, :], ident_s[:]),
                            waits=[tadd, exit_free[half], osc_free[t % 2]], inc="pe")
                    tv = kb.emit("act", lambda e, pt_=pt_, ob=ob, half=half: e.activation(
                        ob[:, half * 512:(half + 1) * 512], pt_[:], AF.Copy),
                        waits=[tp], inc="act")
                    exit_free[half] = tv
                td = kb.emit("po", lambda e, ob=ob, t=t: e.dma_start(
                    out_t[t * 128:(t + 1) * 128, :], ob),
                    waits=[tv], inc="dO%d" % (t % 2), inc_n=16)
                osc_free[t % 2] = td
    kb.wait_only("po", [kb.tok("dO0"), kb.tok("dO1")])
    kb.run("ph45")

    for eng in ENGS:
        for sem, val in kb.waited[eng].items():
            assert val <= kb.c[sem], (eng, sem, val, kb.c[sem])
    for name in list(guards):
        try:
            guards.pop(name).__exit__(None, None, None)
        except Exception:
            pass
    return nc


def prep_inputs(inp):
    f = lambda a: np.ascontiguousarray(np.asarray(a, dtype=np.float32))
    cc = np.ascontiguousarray
    d = {}
    d["w_in"] = f(inp["L0_attn_w_in"])
    d["w_out"] = f(inp["L0_attn_w_out"])
    d["b_qkv"] = cc(f(inp["L0_attn_b_in"]).reshape(24, 128).T)
    d["b_out_c"] = cc(f(inp["L0_attn_b_out"]).reshape(8, 128).T)
    d["cn_w1"] = f(inp["L0_ffn_cn_w1"])
    d["cn_w2"] = f(inp["L0_ffn_cn_w2"])
    d["cn_w3"] = f(inp["L0_ffn_cn_w3"])
    d["cn_w4c"] = cc(f(inp["L0_ffn_cn_w4"]).reshape(2, 128).T)
    d["cn_b1c"] = cc(f(inp["L0_ffn_cn_b1"]).reshape(8, 128).T)
    d["cn_b2c"] = cc(f(inp["L0_ffn_cn_b2"]).reshape(4, 128).T)
    d["cn_b3c"] = cc(f(inp["L0_ffn_cn_b3"]).reshape(2, 128).T)
    d["cn_b4"] = f(inp["L0_ffn_cn_b4"]).reshape(1, 1)
    d["gc_w1"] = f(inp["L0_ffn_gc_w1"])
    d["gc_w2"] = f(inp["L0_ffn_gc_w2"])
    d["gc_w3c"] = cc(f(inp["L0_ffn_gc_w3"]).reshape(2, 128).T)
    d["gc_b1c"] = cc(f(inp["L0_ffn_gc_b1"]).reshape(4, 128).T)
    d["gc_b2c"] = cc(f(inp["L0_ffn_gc_b2"]).reshape(2, 128).T)
    d["gc_b3"] = f(inp["L0_ffn_gc_b3"]).reshape(1, 1)
    d["adrange"] = f(inp["L0_ffn_adaptive_range"]).reshape(1, 1)
    return d


def kernel(**inputs):
    x = np.asarray(inputs["L0_attn_x"], dtype=np.float32)
    B, L, _ = x.shape
    shared = prep_inputs(inputs)
    nc = build_core_program(L=L)
    in_maps = []
    for b in range(B):
        m = dict(shared)
        m["x"] = np.ascontiguousarray(x[b])
        in_maps.append(m)
    res = run_bass_kernel_spmd(nc, in_maps, core_ids=list(range(B)))
    return np.stack([np.asarray(r["out"]) for r in res.results], axis=0)


# version 32 (final)
# speedup vs baseline: 1.0137x; 1.0092x over previous
import math
import numpy as np
import ml_dtypes

import concourse.bass as bass
import concourse.mybir as mybir
from concourse.bass_utils import run_bass_kernel_spmd

F32 = mybir.dt.float32
BF16 = mybir.dt.bfloat16
AF = mybir.ActivationFunctionType
OP = mybir.AluOpType
AX = mybir.AxisListType
BNP = ml_dtypes.bfloat16

D = 1024
H = 16
NCH = 8
NEG = -1.0e30
LN_EPS = 1e-5
ENGS = ("pe", "act", "dve", "po", "sp")


def alibi_slopes(n=H):
    start = 2.0 ** (-(2.0 ** (-(math.log2(n) - 3))))
    return [start * start ** i for i in range(n)]


def split3_bf16(row64):
    hi = row64.astype(BNP)
    r1 = row64 - hi.astype(np.float64)
    lo = r1.astype(BNP)
    lo2 = (r1 - lo.astype(np.float64)).astype(BNP)
    return hi, lo, lo2


def make_consts(L):
    slopes = alibi_slopes()
    k = np.arange(L, dtype=np.float64)
    ramp3 = np.zeros((3 * H, L), dtype=BNP)
    ramp4 = np.zeros((H, L), dtype=BNP)
    for h in range(H):
        hi, lo, lo2 = split3_bf16(slopes[h] * k)
        ramp3[3 * h], ramp3[3 * h + 1], ramp3[3 * h + 2] = hi, lo, lo2
        ramp4[h] = (slopes[h] * k + 36.0).astype(BNP)
    kk = np.arange(128)[:, None]
    qq = np.arange(128)[None, :]
    maskT = np.where(kk <= qq, 0.0, NEG).astype(np.float32)
    maskT01 = np.where(kk <= qq, 1.0, 0.0).astype(BNP)
    return dict(
        ramp3=ramp3, ramp4=ramp4,
        ident=np.eye(128, dtype=np.float32),
        maskT=maskT, maskT01=maskT01,
        minus1=np.full((1, L), -1.0, dtype=BNP),
        ones_c128=np.ones((128, 1), dtype=BNP),
        ones_r128_bf=np.ones((1, 128), dtype=BNP),
        ones_r128_f=np.ones((1, 128), dtype=np.float32),
        ones_r64=np.ones((1, 64), dtype=BNP),
    )


class KB:

    def __init__(self, nc):
        self.nc = nc
        self.s, self.c = {}, {}
        for name in ("pe", "act", "dve", "po"):
            self._mksem(name)
        self.ops = None
        self.waited = {e: {} for e in ENGS}

    def _mksem(self, name):
        if name not in self.s:
            self.s[name] = self.nc.alloc_semaphore("sem_" + name)
            self.c[name] = 0

    def begin(self):
        self.ops = {e: [] for e in ENGS}

    def tok(self, name):
        self._mksem(name)
        return (name, self.c[name])

    def emit(self, eng, fn, waits=(), inc=None, inc_n=1):
        ws = []
        for w in waits:
            if w is None:
                continue
            sem, val = w
            if val <= 0 or self.waited[eng].get(sem, -1) >= val:
                continue
            self.waited[eng][sem] = val
            ws.append((self.s[sem], val))
        incs = []
        tk = None
        if inc is not None:
            self._mksem(inc)
            self.c[inc] += inc_n
            incs = [(self.s[inc], inc_n)]
            tk = (inc, self.c[inc])
        self.ops[eng].append((fn, ws, incs))
        return tk

    def wait_only(self, eng, waits):
        self.emit(eng, None, waits=waits)

    def run(self, name):
        ops, self.ops = self.ops, None
        with self.nc.Block(name) as block:
            sect = dict(pe=block.tensor, act=block.scalar, dve=block.vector,
                        po=block.gpsimd, sp=block.sync)
            for eng in ENGS:
                lst = ops[eng]
                if not lst:
                    continue

                def body(e, lst=lst):
                    for fn, ws, incs in lst:
                        for (sem, val) in ws:
                            e.wait_ge(sem, val)
                        if fn is None:
                            continue
                        inst = fn(e)
                        for (sem, n) in incs:
                            inst.then_inc(sem, n)
                sect[eng](body)


def build_core_program(L=2048, taps=(), phases=5):
    NT = L // 128
    NU = L // 512
    NQ = L // 512
    NF = L // 256
    NP = H // 2

    nc = bass.Bass("TRN2", target_bir_lowering=False, detect_race_conditions=False)
    kb = KB(nc)
    C = make_consts(L)

    din = {}
    for nm, shp in (("x", [L, D]), ("w_in", [D, 3 * D]), ("w_out", [D, D]),
                    ("b_qkv", [128, 24]), ("b_out_c", [128, 8]),
                    ("cn_w1", [D, D]), ("cn_w2", [D, 512]), ("cn_w3", [512, 256]),
                    ("cn_w4c", [128, 2]), ("cn_b1c", [128, 8]), ("cn_b2c", [128, 4]),
                    ("cn_b3c", [128, 2]), ("cn_b4", [1, 1]),
                    ("gc_w1", [D, 512]), ("gc_w2", [512, 256]), ("gc_w3c", [128, 2]),
                    ("gc_b1c", [128, 4]), ("gc_b2c", [128, 2]), ("gc_b3", [1, 1]),
                    ("adrange", [1, 1])):
        din[nm] = nc.dram_tensor(nm, shp, F32, kind="ExternalInput").ap()
    out_t = nc.dram_tensor("out", [L, D], F32, kind="ExternalOutput").ap()
    xspill = nc.dram_tensor("xspill", [128, NT * NCH * 128], F32, kind="Internal").ap()
    cst = {k: nc.inline_tensor(np.ascontiguousarray(v), "c_" + k).ap()
           for k, v in C.items()}
    tap_aps = {}
    for tname, shape, dt in taps:
        tap_aps[tname] = nc.dram_tensor(tname, shape, dt, kind="ExternalOutput").ap()

    guards = {}

    def sb(name, shape, dtype):
        g = nc.sbuf_tensor(name, shape, dtype)
        t = g.__enter__()
        guards[name] = g
        return t

    def sbfree(*names):
        for name in names:
            guards.pop(name).__exit__(None, None, None)

    def ps(name, shape):
        g = nc.psum_tensor(name, shape, F32)
        t = g.__enter__()
        guards[name] = g
        return t

    def fms(tens, t0, nt, c, p0=0, np_=128):
        return tens[p0:p0 + np_, t0:t0 + nt, c, :]

    def r3(ap2d, b=128):
        return ap2d.rearrange("p (a b) -> p a b", b=b)

    ident_s = sb("ident_s", [128, 128], F32)
    maskT_s = sb("maskT_s", [128, 128], F32)
    mask01_s = sb("mask01_s", [128, 128], BF16)
    onesc_s = sb("onesc_s", [128, 1], BF16)
    onesrb_s = sb("onesrb_s", [1, 128], BF16)
    onesrf_s = sb("onesrf_s", [1, 128], F32)
    onesr64_s = sb("onesr64_s", [1, 64], BF16)
    bqkv_s = sb("bqkv_s", [128, 24], F32)
    bksc_s = sb("bksc_s", [128, 8], F32)
    bout_s = sb("bout_s", [128, 8], F32)
    cnb1_s = sb("cnb1_s", [128, 8], F32)
    cnb2_s = sb("cnb2_s", [128, 4], F32)
    cnb3_s = sb("cnb3_s", [128, 2], F32)
    cnw4_s = sb("cnw4_s", [128, 2], BF16)
    gcb1_s = sb("gcb1_s", [128, 4], F32)
    gcb2_s = sb("gcb2_s", [128, 2], F32)
    gcw3_s = sb("gcw3_s", [128, 2], BF16)
    scal_s = sb("scal_s", [1, 16], F32)
    gcr_s = sb("gcr_s", [128, 8], BF16)
    gcp_s = sb("gcp_s", [128, 32], F32)
    gch1_s = sb("gch1_s", [128, 4], BF16)
    gch2_s = sb("gch2_s", [128, 2], BF16)
    rows_s = sb("rows_s", [1, 4096], F32)
    aux_s = sb("aux_s", [1, 1024], F32)
    rowb_s = sb("rowb_s", [1, 1024], BF16)
    lnt_s = sb("lnt_s", [128, 2048], F32)
    wscr_s = sb("wscr_s", [128, 2048], F32)
    wscr2_s = sb("wscr2_s", [128, 2048], F32)
    xb0_s = sb("xb0_s", [128, 512], BF16)
    xb1_s = sb("xb1_s", [128, 512], BF16)
    sq0_s = sb("sq0_s", [128, 512], BF16)
    sq1_s = sb("sq1_s", [128, 512], BF16)
    big_s = sb("big_s", [128, NT, NCH, 128], BF16)

    pA = ps("pA", [128, 512])
    pB = ps("pB", [128, 512])
    pO0 = ps("pO0", [128, 512])
    pO1 = ps("pO1", [128, 512])
    pR0 = ps("pR0", [128, 512])
    pR1 = ps("pR1", [128, 512])
    pRowA = ps("pRowA", [128, 512])
    pRowB = ps("pRowB", [128, 512])

    kb.begin()
    for dst, src in ((ident_s, cst["ident"]), (maskT_s, cst["maskT"]),
                     (mask01_s, cst["maskT01"]),
                     (onesc_s, cst["ones_c128"]), (onesrb_s, cst["ones_r128_bf"]),
                     (onesrf_s, cst["ones_r128_f"]), (onesr64_s, cst["ones_r64"]),
                     (bqkv_s, din["b_qkv"]), (bout_s, din["b_out_c"]),
                     (cnb1_s, din["cn_b1c"]), (cnb2_s, din["cn_b2c"]),
                     (cnb3_s, din["cn_b3c"]), (gcb1_s, din["gc_b1c"]),
                     (gcb2_s, din["gc_b2c"])):
        kb.emit("sp", lambda e, d=dst, s=src: e.dma_start(d[:], s[:]),
                inc="dC", inc_n=16)
    kb.emit("sp", lambda e: e.dma_start(scal_s[0:1, 8:9], din["cn_b4"][:]), inc="dC", inc_n=16)
    kb.emit("sp", lambda e: e.dma_start(scal_s[0:1, 9:10], din["gc_b3"][:]), inc="dC", inc_n=16)
    kb.emit("sp", lambda e: e.dma_start(scal_s[0:1, 2:3], din["adrange"][:]), inc="dC", inc_n=16)
    kb.emit("sp", lambda e: e.dma_start(wscr_s[:, 0:2], din["cn_w4c"][:]), inc="dC", inc_n=16)
    kb.emit("sp", lambda e: e.dma_start(wscr_s[:, 2:4], din["gc_w3c"][:]), inc="dC", inc_n=16)
    tall = kb.tok("dC")
    kb.emit("dve", lambda e: e.tensor_copy(cnw4_s[:], wscr_s[:, 0:2]), waits=[tall])
    kb.emit("dve", lambda e: e.tensor_copy(gcw3_s[:], wscr_s[:, 2:4]), waits=[tall])
    kb.emit("act", lambda e: e.mul(bksc_s[:], bqkv_s[:, 8:16], 0.125), waits=[tall])
    kb.emit("act", lambda e: e.mul(scal_s[0:1, 0:1], scal_s[0:1, 8:9], 0.5), waits=[tall])
    kb.emit("act", lambda e: e.mul(scal_s[0:1, 1:2], scal_s[0:1, 9:10], 0.5), waits=[tall])
    kb.run("ph0")

    wqb = (sb("wqb0", [128, 1024], BF16), sb("wqb1", [128, 1024], BF16))
    wkb = (sb("wkb0", [128, 1024], BF16), sb("wkb1", [128, 1024], BF16))
    wvb = (sb("wvb0", [128, 1024], BF16), sb("wvb1", [128, 1024], BF16))
    wq_ready = [None, None]
    wchain_state = [None, 0, None]

    def emit_wchain(cp):
        pp = cp % 2
        ct = None
        for wi, (wb, coff) in enumerate(((wqb, 0), (wkb, D), (wvb, 2 * D))):
            gi = wchain_state[1]
            wchain_state[1] += 1
            scr = (wscr_s, wscr2_s)[gi % 2]
            col = coff + cp * 128
            td = kb.emit("sp", lambda e, col=col, scr=scr: e.dma_start(
                scr[:, 0:1024].rearrange("p (k j) -> p k j", j=128),
                din["w_in"][:, col:col + 128].rearrange("(k p) j -> p k j", p=128)),
                waits=[wchain_state[0] if gi % 2 == 0 else wchain_state[2]],
                inc="dWin%d" % (gi % 2), inc_n=16)
            ct = kb.emit("po", lambda e, wb=wb, pp=pp, scr=scr: e.tensor_copy(
                wb[pp][:], scr[:, 0:1024]), waits=[td, w_free[pp]], inc="po")
            if gi % 2 == 0:
                wchain_state[0] = ct
            else:
                wchain_state[2] = ct
        wq_ready[pp] = ct

    w_free = [None, None]
    ln1T = sb("ln1T", [128, NT, NCH, 128], BF16)
    xT = sb("xT", [128, NT, NCH, 128], F32)
    land = (sb("land0", [128, 1024], F32), sb("land1", [128, 1024], F32))

    kb.begin()
    land_free = [None, None]
    tile_done = {}
    for t in range(NT):
        lb = land[t % 2]
        pp = ((pA, pB), (pO0, pO1))[t % 2]
        tdma = kb.emit("sp", lambda e, lb=lb, t=t: e.dma_start(
            lb[:], din["x"][t * 128:(t + 1) * 128, :]),
            waits=[land_free[t % 2]], inc="dL%d" % (t % 2), inc_n=16)
        for half in range(2):
            pt = pp[half]
            tp = None
            for cc in range(4):
                c = half * 4 + cc
                tp = kb.emit("pe", lambda e, pt=pt, cc=cc, lb=lb, c=c: e.transpose(
                    pt[:, cc * 128:(cc + 1) * 128], lb[:, c * 128:(c + 1) * 128],
                    ident_s[:]), waits=[tdma, tile_done.get(t - 2)], inc="pe")
            tv = kb.emit("dve", lambda e, pt=pt, t=t, half=half: e.tensor_copy(
                xT[:, t, half * 4:half * 4 + 4, :], r3(pt[:])),
                waits=[tp], inc="dve")
        tile_done[t] = tv
        land_free[t % 2] = tp
    emit_wchain(0)
    emit_wchain(1)
    spill_last = None
    ln1_done = {}
    rowA_free = rowB_free = None
    for u in range(NU):
        tx = tile_done[4 * u + 3]
        mm_s = mm_q = None
        for c in range(NCH):
            xb = (xb0_s, xb1_s)[c % 2]
            sq = (sq0_s, sq1_s)[c % 2]
            tc = kb.emit("act", lambda e, xb=xb, u=u, c=c: e.activation(
                r3(xb[:]), fms(xT, 4 * u, 4, c), AF.Copy), waits=[tx, mm_s], inc="act")
            tq = kb.emit("po", lambda e, xb=xb, sq=sq: e.tensor_tensor(
                sq[:], xb[:], xb[:], OP.mult), waits=[tc], inc="po")
            mm_s = kb.emit("pe", lambda e, xb=xb, c=c: e.matmul(
                pRowA[0:1, :], onesc_s[:], xb[:], start=(c == 0), stop=(c == NCH - 1)),
                waits=[tc, rowA_free], inc="pe")
            mm_q = kb.emit("pe", lambda e, sq=sq, c=c: e.matmul(
                pRowB[0:1, :], onesc_s[:], sq[:], start=(c == 0), stop=(c == NCH - 1)),
                waits=[tq, rowB_free], inc="pe")
        m_bf, r_bf = rowb_s[0:1, 0:512], rowb_s[0:1, 512:1024]
        a0, a1 = aux_s[0:1, 0:512], aux_s[0:1, 512:1024]
        t1 = kb.emit("dve", lambda e, m_bf=m_bf: e.tensor_scalar(
            m_bf[:], pRowA[0:1, :], 1.0 / D, None, OP.mult), waits=[mm_s], inc="dve")
        t2 = kb.emit("dve", lambda e, a0=a0, m_bf=m_bf: e.tensor_tensor(
            a0[:], m_bf[:], m_bf[:], OP.mult), waits=[t1], inc="dve")
        t3 = kb.emit("dve", lambda e, a1=a1: e.tensor_scalar(
            a1[:], pRowB[0:1, :], 1.0 / D, LN_EPS, OP.mult, OP.add), waits=[mm_q], inc="dve")
        t4 = kb.emit("dve", lambda e, a0=a0, a1=a1: e.tensor_tensor(
            a0[:], a1[:], a0[:], OP.subtract), waits=[t2, t3], inc="dve")
        rowA_free, rowB_free = t1, t3
        t5 = kb.emit("act", lambda e, a0=a0, a1=a1: e.activation(
            a1[:], a0[:], AF.Ln), waits=[t4], inc="act")
        t6 = kb.emit("act", lambda e, a1=a1, r_bf=r_bf: e.activation(
            r_bf[:], a1[:], AF.Exp, scale=-0.5), waits=[t5], inc="act")
        bm = kb.emit("pe", lambda e, m_bf=m_bf: e.matmul(
            pR0[:], onesrb_s[:], m_bf[:], start=True, stop=True),
            waits=[t1, ln1_done.get(u - 1)], inc="pe")
        br = kb.emit("pe", lambda e, r_bf=r_bf: e.matmul(
            pR1[:], onesrb_s[:], r_bf[:], start=True, stop=True),
            waits=[t6], inc="pe")
        tn = None
        for ch in range(2):
            c0 = ch * 4
            lv = lnt_s[:].rearrange("p (a b j) -> p a b j", b=4, j=128)
            bc0 = r3(pR0[:]).unsqueeze(2).broadcast_to([128, 4, 4, 128])
            bc1 = r3(pR1[:]).unsqueeze(2).broadcast_to([128, 4, 4, 128])
            tt1 = kb.emit("dve", lambda e, u=u, c0=c0, lv=lv, bc0=bc0: e.tensor_tensor(
                lv, xT[:, 4 * u:4 * u + 4, c0:c0 + 4, :], bc0, OP.subtract),
                waits=[bm, br], inc="dve")
            tn = kb.emit("dve", lambda e, u=u, c0=c0, lv=lv, bc1=bc1: e.tensor_tensor(
                ln1T[:, 4 * u:4 * u + 4, c0:c0 + 4, :], lv, bc1, OP.mult),
                waits=[tt1], inc="dve")
        ln1_done[u] = tn
        spill_last = kb.emit("sp", lambda e, u=u: e.dma_start(
            xspill[:, u * 4096:(u + 1) * 4096].rearrange(
                "p (a b c) -> p a b c", b=NCH, c=128),
            xT[:, 4 * u:4 * u + 4, :, :]), waits=[tn], inc="dSp", inc_n=16)
    kb.wait_only("sp", [kb.tok("dSp")])
    kb.run("ph1")
    if phases <= 1:
        return nc

    if "tap_ln1T" in tap_aps:
        kb.begin()
        kb.emit("po", lambda e: e.dma_start(
            tap_aps["tap_ln1T"][:].rearrange("p (t c j) -> p t c j", c=NCH, j=128),
            ln1T[:, :, :, :]), inc="dT", inc_n=16)
        kb.wait_only("po", [kb.tok("dT")])
        kb.run("ph1t")

    sbfree("land1", "land0", "xT")

    qh = ((sb("qh00", [128, L], BF16), sb("qh01", [128, L], BF16)),
          (sb("qh10", [128, L], BF16), sb("qh11", [128, L], BF16)))
    kh = ((sb("kh00", [128, L], BF16), sb("kh01", [128, L], BF16)),
          (sb("kh10", [128, L], BF16), sb("kh11", [128, L], BF16)))
    vfm = (sb("vfm0", [128, L], BF16), sb("vfm1", [128, L], BF16))
    vt = ((sb("vt00", [128, NT, 128], BF16), sb("vt01", [128, NT, 128], BF16)),
          (sb("vt10", [128, NT, 128], BF16), sb("vt11", [128, NT, 128], BF16)))
    PT = tuple(sb("pt%d" % i, [128, 512], BF16) for i in range(5))
    rsb = sb("rsb", [128, 512], BF16)
    pQK = (pR1,)

    kb.begin()
    spill_done = kb.tok("dSp")
    kb.wait_only("pe", [ln1_done[NU - 1]])
    init_dve = None
    for pr in qh:
        for t_ in pr:
            init_dve = kb.emit("dve", lambda e, t_=t_: e.memset(t_[64:67, :], 1.0),
                               waits=[spill_done], inc="dve")
    for pr in vt:
        init_dve = kb.emit("dve", lambda e, pr=pr: e.memset(pr[0][:, :, 64:65], 1.0), inc="dve")
        init_dve = kb.emit("dve", lambda e, pr=pr: e.memset(pr[1][:, :, 0:1], 1.0), inc="dve")
        init_dve = kb.emit("dve", lambda e, pr=pr: e.memset(pr[1][:, :, 1:64], 0.0), inc="dve")
    init_dg = [[None, None], [None, None]]
    for pp in range(2):
        for h01 in range(2):
            init_dg[pp][h01] = kb.emit("po", lambda e, t_=kh[pp][h01]: e.dma_start(
                t_[67:68, :], cst["minus1"][:]), waits=[spill_done],
                inc="dK%d%d" % (pp, h01), inc_n=16)

    q_free = [[None, None], [None, None]]
    v_free = [[None, None], [None, None]]
    vfm_free = [[None] * NU, [None] * NU]
    vtr_tok = {}
    pt_free = [None] * 5
    pS_free = [None, None, None]
    gkt = [0]
    pO_free = [None, None]
    pQK_free = [None, None]
    pR0_free = [None]
    rsb_free = [None]
    r_free = [None, None]
    escr_free = [None, None]
    evq = [[None], [None]]
    evk = [[None], [None]]
    evv = [[None], [None]]
    head_tok_p = [None, None]
    wscr_free = None
    pending_evac = None
    pending_pvs = []
    qkv_gi = [0]

    def qkv_filler(cp):
        pp = cp % 2
        steps = []
        for mi, wb in enumerate((wqb, wkb, wvb)):
            for u in range(NU):
                gi = qkv_gi[0]
                qkv_gi[0] += 1
                last = {}
                for kc in range(NCH):
                    def mm_step(kc=kc, gi=gi, wb=wb, pp=pp, u=u, last=last):
                        pg = pQK[0]
                        last["mm"] = kb.emit("pe", lambda e, pg=pg, wb=wb, pp=pp, kc=kc, u=u: e.matmul(
                            pg[:], wb[pp][:, kc * 128:(kc + 1) * 128],
                            fms(ln1T, 4 * u, 4, kc),
                            start=(kc == 0), stop=(kc == NCH - 1)),
                            waits=[wq_ready[pp], pQK_free[0]], inc="pe")
                    steps.append(mm_step)
                def evac_step(mi=mi, u=u, gi=gi, pp=pp, cp=cp, last=last):
                    pg = pQK[0]
                    mm = last["mm"]
                    if mi < 2:
                        dst = qh[pp] if mi == 0 else kh[pp]
                        scr = (xb0_s, xb1_s)[gi % 2]
                        if mi == 0:
                            tv = kb.emit("dve", lambda e, pg=pg, scr=scr, cp=cp: e.tensor_scalar(
                                scr[:], pg[:], bqkv_s[:, cp:cp + 1], None, OP.add),
                                waits=[mm, escr_free[gi % 2]], inc="dve")
                        else:
                            tv = kb.emit("dve", lambda e, pg=pg, scr=scr, cp=cp: e.tensor_scalar(
                                scr[:], pg[:], 0.125, bksc_s[:, cp:cp + 1],
                                OP.mult, OP.add),
                                waits=[mm, escr_free[gi % 2]], inc="dve")
                        tsh = None
                        for h01 in range(2):
                            tsh = kb.emit("po", lambda e, scr=scr, h01=h01, dst=dst, u=u: e.dma_start(
                                dst[h01][0:64, u * 512:(u + 1) * 512],
                                scr[h01 * 64:h01 * 64 + 64, :]),
                                waits=[tv, q_free[pp][h01]],
                                inc="dE%d" % (gi % 2), inc_n=16)
                        escr_free[gi % 2] = tsh
                        (evq if mi == 0 else evk)[pp][0] = tsh
                        pQK_free[0] = tv
                    else:
                        tv = kb.emit("dve", lambda e, pg=pg, pp=pp, u=u, cp=cp: e.tensor_scalar(
                            vfm[pp][:, u * 512:(u + 1) * 512], pg[:],
                            bqkv_s[:, 16 + cp:17 + cp], None, OP.add),
                            waits=[mm, vfm_free[pp][u]], inc="dve")
                        evv[pp][0] = tv
                        pQK_free[0] = tv
                        last_t = None
                        for h01 in range(2):
                            sem = "dV%d%d%d" % (pp, h01, u)
                            for tti in range(4):
                                ti = 4 * u + tti
                                c0 = 0 if h01 == 0 else 64
                                qe = ("sp", "act")[tti % 2]
                                last_t = kb.emit(qe, lambda e, h01=h01, ti=ti, c0=c0, pp=pp: e.dma_start_transpose(
                                    vt[pp][h01][:, ti, c0:c0 + 64],
                                    vfm[pp][h01 * 64:h01 * 64 + 64, ti * 128:(ti + 1) * 128]),
                                    waits=[tv, v_free[pp][h01]],
                                    inc=sem, inc_n=16)
                            vtr_tok[(pp, h01, u)] = last_t
                        vfm_free[pp][u] = last_t
                steps.append(evac_step)
        return steps

    def bprep(cp):
        pp = cp % 2
        toks = []
        for h01 in range(2):
            h = 2 * cp + h01
            tr3 = kb.emit("po", lambda e, h=h, h01=h01, pp=pp: e.dma_start(
                kh[pp][h01][64:67, :], cst["ramp3"][3 * h:3 * h + 3, :]),
                waits=[q_free[pp][h01], evk[pp][0]], inc="dK%d%d" % (pp, h01), inc_n=16)
            tbp = kb.emit("po", lambda e, h=h, h01=h01, pp=pp: e.dma_start(
                qh[pp][h01][67:68, :], cst["ramp4"][h:h + 1, :]),
                waits=[q_free[pp][h01], evq[pp][0]], inc="dB%d%d" % (pp, h01), inc_n=16)
            toks.append((tr3, tbp))
        head_tok_p[pp] = toks

    for st in qkv_filler(0):
        st()
    bprep(0)

    for cp in range(NP):
        pp = cp % 2
        filler = qkv_filler(cp + 1) if cp + 1 < NP else []
        if cp + 2 < NP:
            emit_wchain(cp + 2)
        nblocks = 2 * NQ
        bi = 0
        for h01 in range(2):
            tr3, tbp = head_tok_p[pp][h01]
            last_mm = None
            for t in range(NQ):
                oi = (h01 * NQ + t) % 2
                pO = (pO0, pO1)[oi]
                Wh = 94.0 / alibi_slopes()[2 * cp + h01]
                kts = [4 * t + j for j in range(4)] + [
                    kt for kt in range(4 * t) if 128 * kt + 127 >= 512 * t - Wh]
                nk = len(kts)
                exps = [None] * nk
                def emit_pv(i, pO=pO, pp=pp, h01=h01, kts=None, nk=0, exps=None, t=t, oi=oi):
                    kt = kts[i]
                    j = kt - 4 * t if kt >= 4 * t else None
                    col0 = 128 * j if j is not None else 0
                    etok, pti = exps[i]
                    mrange = 65 if h01 == 0 else 128
                    tok = kb.emit("pe", lambda e, pO=pO, pti=pti, col0=col0, h01=h01, kt=kt, i=i, nk=nk, mrange=mrange, pp=pp: e.matmul(
                        pO[0:mrange, col0:512], vt[pp][h01][:, kt, 0:mrange], PT[pti][:, col0:512],
                        start=(i == 0), stop=(i == nk - 1)),
                        waits=[etok, vtr_tok[(pp, h01, kt // 4)], init_dve,
                               pO_free[oi] if i == 0 else None], inc="pe")
                    return (tok, pti)
                last_pv = None
                for i, kt in enumerate(kts):
                    si = gkt[0] % 3
                    pS = (pA, pB, pRowB)[si]
                    j = kt - 4 * t if kt >= 4 * t else None
                    col0 = 128 * j if j is not None else 0
                    mm = kb.emit("pe", lambda e, pS=pS, kt=kt, t=t, col0=col0, h01=h01, pp=pp: e.matmul(
                        pS[:, col0:512], kh[pp][h01][0:68, kt * 128:(kt + 1) * 128],
                        qh[pp][h01][0:68, t * 512 + col0:(t + 1) * 512],
                        start=True, stop=True),
                        waits=[tr3, tbp, init_dg[pp][h01], pS_free[si]],
                        inc="pe")
                    last_mm = mm
                    if i in (1, 2, 3) and pending_pvs:
                        pending_pvs.pop(0)()
                    if i == 4 and pending_evac is not None:
                        pending_evac()
                        pending_evac = None
                    pti = gkt[0] % 5
                    gkt[0] += 1
                    texp = kb.emit("act", lambda e, pS=pS, pti=pti, col0=col0: e.activation(
                        PT[pti][:, col0:512], pS[:, col0:512], AF.Exp),
                        waits=[mm, pt_free[pti]], inc="act")
                    if j is not None:
                        texp = kb.emit("dve", lambda e, pti=pti, col0=col0: e.tensor_tensor(
                            PT[pti][:, col0:col0 + 128], PT[pti][:, col0:col0 + 128],
                            mask01_s[:], OP.mult), waits=[texp], inc="dve")
                    exps[i] = (texp, pti)
                    pS_free[si] = texp
                    if i >= 2:
                        last_pv, lpti = emit_pv(i - 2, kts=kts, nk=nk, exps=exps)
                        pt_free[lpti] = last_pv
                    if filler:
                        filler.pop(0)()
                if pending_evac is not None:
                    pending_evac()
                    pending_evac = None
                tail_state = {}
                def mk_tail(ii, kts=kts, nk=nk, exps=exps, pp=pp, h01=h01, tail_state=tail_state, emit_pv=emit_pv):
                    def go():
                        tok, lpti = emit_pv(ii, kts=kts, nk=nk, exps=exps)
                        pt_free[lpti] = tok
                        tail_state["last"] = tok
                        if ii == nk - 1:
                            v_free[pp][h01] = tok
                    return go
                for ii in (nk - 2, nk - 1):
                    pending_pvs.append(mk_tail(ii))
                bi += 1
                while len(filler) > 30 + 12 * (nblocks - bi):
                    filler.pop(0)()
                srow = 64 if h01 == 0 else 0
                a_r = aux_s[0:1, 512:1024]
                r_bf = rowb_s[0:1, oi * 512:(oi + 1) * 512]
                def mk_parta(pO=pO, oi=oi, srow=srow, a_r=a_r, r_bf=r_bf,
                             tail_state=tail_state):
                    def go():
                        lpv = tail_state["last"]
                        tl = kb.emit("act", lambda e, pO=pO, srow=srow, a_r=a_r: e.activation(
                            a_r[:], pO[srow:srow + 1, 0:512], AF.Ln),
                            waits=[lpv, r_free[oi]], inc="act")
                        def _rexp(e, a_r=a_r, r_bf=r_bf):
                            with nc.allow_low_precision(reason="1/s as bf16 operand"):
                                return e.activation(r_bf[:], a_r[:], AF.Exp, scale=-1.0)
                        tail_state["tre"] = kb.emit("act", _rexp, waits=[tl], inc="act")
                    return go
                pending_pvs.append(mk_parta())
                def mk_evac(pO=pO, oi=oi, h01=h01, t=t, cp=cp, r_bf=r_bf,
                            tail_state=tail_state):
                    def go():
                        tre = tail_state["tre"]
                        tRm = kb.emit("pe", lambda e, r_bf=r_bf: e.matmul(
                            pR0[:], onesrb_s[:], r_bf[:], start=True, stop=True),
                            waits=[tre, pR0_free[0]], inc="pe")
                        r_free[oi] = tRm
                        tRe = kb.emit("dve", lambda e: e.tensor_copy(
                            rsb[:], pR0[:]), waits=[tRm, rsb_free[0]], inc="dve")
                        pR0_free[0] = tRe
                        p0 = h01 * 64
                        tev = kb.emit("dve", lambda e, pO=pO, t=t, cp=cp, p0=p0: e.tensor_tensor(
                            fms(big_s, 4 * t, 4, cp, p0, 64),
                            r3(pO[p0:p0 + 64, 0:512]), r3(rsb[p0:p0 + 64, :]), OP.mult),
                            waits=[tRe], inc="dve")
                        pO_free[oi] = tev
                        rsb_free[0] = tev
                    return go
                pending_evac = mk_evac()
            q_free[pp][h01] = last_mm
        for st in pending_pvs:
            st()
        pending_pvs = []
        for st in filler:
            st()
        if cp + 1 < NP:
            bprep(cp + 1)
    if pending_evac is not None:
        pending_evac()
        pending_evac = None
    kb.run("ph2")
    if phases <= 2:
        return nc

    if "tap_aout" in tap_aps:
        kb.begin()
        kb.emit("po", lambda e: e.dma_start(
            tap_aps["tap_aout"][:].rearrange("p (t c j) -> p t c j", c=NCH, j=128),
            big_s[:, :, :, :]), inc="dT", inc_n=16)
        kb.wait_only("po", [kb.tok("dT")])
        kb.run("ph2t")

    sbfree("rsb", "pt4", "pt3", "pt2", "pt1", "pt0",
           "vt11", "vt10", "vt01", "vt00", "vfm1", "vfm0",
           "kh11", "kh10", "kh01", "kh00", "qh11", "qh10", "qh01", "qh00",
           "ln1T", "wvb1", "wvb0", "wkb1", "wkb0", "wqb1", "wqb0")

    xT2 = sb("xT2", [128, NT, NCH, 128], F32)
    cn1b = sb("cn1b", [128, 8, 1024], BF16)
    cn2b = sb("cn2b", [128, 8, 512], BF16)
    cn3b = sb("cn3b", [128, 4, 256], BF16)
    gc1b = sb("gc1b", [128, 8, 512], BF16)
    gc2b = sb("gc2b", [128, 4, 256], BF16)
    woutb = sb("woutb", [128, 8, 1024], BF16)

    kb.begin()
    chains = [(woutb, din["w_out"], NCH, 1024), (cn1b, din["cn_w1"], 8, 1024),
              (gc1b, din["gc_w1"], 8, 512), (cn2b, din["cn_w2"], 8, 512),
              (gc2b, din["gc_w2"], 4, 256), (cn3b, din["cn_w3"], 4, 256)]
    wq_casts = {}
    gidx = 0
    cast_free = [wq_ready[1], wq_ready[1]]
    for dst, wsrc, nkc, ncol in chains:
        for kc0 in range(0, nkc, 2):
            nk2 = min(2, nkc - kc0)
            scr = (wscr_s, wscr2_s)[gidx % 2]
            qe = ("sp", "po")[gidx % 2]
            td = kb.emit(qe, lambda e, wsrc=wsrc, kc0=kc0, nk2=nk2, ncol=ncol, scr=scr: e.dma_start(
                scr[:, 0:nk2 * ncol].rearrange("p (a b) -> p a b", b=ncol),
                wsrc[kc0 * 128:(kc0 + nk2) * 128, :].rearrange(
                    "(a p) b -> p a b", p=128)),
                waits=[cast_free[gidx % 2]], inc="dW%d" % (gidx % 2), inc_n=16)
            ct = kb.emit("act", lambda e, dst=dst, kc0=kc0, nk2=nk2, ncol=ncol, scr=scr: e.activation(
                dst[:, kc0:kc0 + nk2, 0:ncol],
                scr[:, 0:nk2 * ncol].rearrange("p (a b) -> p a b", b=ncol),
                AF.Copy), waits=[td], inc="act")
            cast_free[gidx % 2] = ct
            gidx += 1
        wq_casts[dst.name] = ct
    wcast = wq_casts["woutb"]
    ffnw_tok = ct
    xload = []
    for u in range(NU):
        xload.append(kb.emit("sp", lambda e, u=u: e.dma_start(
            xT2[:, 4 * u:4 * u + 4, :, :],
            xspill[:, u * 4096:(u + 1) * 4096].rearrange(
                "p (a b c) -> p a b c", b=NCH, c=128)), inc="dX", inc_n=16))
    pY_free = [None, None, None, None]
    tev_tok = {}
    for m in range(NCH):
        for u in range(NU):
            yi = (m * NU + u) % 4
            pY = (pA, pB, pO0, pO1)[yi]
            mm = None
            for kc in range(NCH):
                mm = kb.emit("pe", lambda e, pY=pY, kc=kc, m=m, u=u: e.matmul(
                    pY[:], woutb[:, kc, m * 128:(m + 1) * 128],
                    fms(big_s, 4 * u, 4, kc),
                    start=(kc == 0), stop=(kc == NCH - 1)),
                    waits=[wcast, pY_free[yi]], inc="pe")
            tev = kb.emit("dve", lambda e, pY=pY, m=m, u=u: e.scalar_tensor_tensor(
                fms(xT2, 4 * u, 4, m), r3(pY[:]), bout_s[:, m:m + 1],
                fms(xT2, 4 * u, 4, m), OP.add, OP.add),
                waits=[mm, xload[-1]], inc="dve")
            tev_tok[(m, u)] = tev
            pY_free[yi] = tev
    kb.run("ph3")
    if phases <= 3:
        return nc

    if "tap_x2" in tap_aps:
        kb.begin()
        kb.emit("po", lambda e: e.dma_start(
            tap_aps["tap_x2"][:].rearrange("p (t c j) -> p t c j", c=NCH, j=128),
            xT2[:, :, :, :]), inc="dT", inc_n=16)
        kb.wait_only("po", [kb.tok("dT")])
        kb.run("ph3t")

    sbfree("woutb")

    outT = sb("outT", [128, 8, 256], F32)
    h1b = sb("h1b", [128, 8, 256], BF16)
    h2b = sb("h2b", [128, 4, 256], BF16)
    h3b = sb("h3b", [128, 2, 256], BF16)
    osc = (wscr_s[:, 0:1024], wscr2_s[:, 0:1024])

    kb.begin()
    rowA_free = rowB_free = None
    ln2_done = {}
    ln2_rows = {}
    gc_last = None
    for u in range(NU):
        mm_s = mm_q = None
        for c in range(NCH):
            xb = (xb0_s, xb1_s)[c % 2]
            sq = (sq0_s, sq1_s)[c % 2]
            tc = kb.emit("act", lambda e, xb=xb, u=u, c=c: e.activation(
                r3(xb[:]), fms(xT2, 4 * u, 4, c), AF.Copy),
                waits=[mm_s, tev_tok[(c, u)]], inc="act")
            tq = kb.emit("po", lambda e, xb=xb, sq=sq: e.tensor_tensor(
                sq[:], xb[:], xb[:], OP.mult), waits=[tc], inc="po")
            mm_s = kb.emit("pe", lambda e, xb=xb, c=c: e.matmul(
                pRowA[0:1, :], onesc_s[:], xb[:], start=(c == 0), stop=(c == NCH - 1)),
                waits=[tc, rowA_free], inc="pe")
            mm_q = kb.emit("pe", lambda e, sq=sq, c=c: e.matmul(
                pRowB[0:1, :], onesc_s[:], sq[:], start=(c == 0), stop=(c == NCH - 1)),
                waits=[tq, rowB_free], inc="pe")
        m_f = rows_s[0:1, u * 512:(u + 1) * 512]
        r_f = rows_s[0:1, 2048 + u * 512:2048 + (u + 1) * 512]
        a0, a1 = aux_s[0:1, 0:512], aux_s[0:1, 512:1024]
        t1 = kb.emit("dve", lambda e, m_f=m_f: e.tensor_scalar(
            m_f[:], pRowA[0:1, :], 1.0 / D, None, OP.mult), waits=[mm_s], inc="dve")
        t2 = kb.emit("dve", lambda e, a0=a0, m_f=m_f: e.tensor_tensor(
            a0[:], m_f[:], m_f[:], OP.mult), waits=[t1], inc="dve")
        t3 = kb.emit("dve", lambda e, a1=a1: e.tensor_scalar(
            a1[:], pRowB[0:1, :], 1.0 / D, LN_EPS, OP.mult, OP.add), waits=[mm_q], inc="dve")
        t4 = kb.emit("dve", lambda e, a0=a0, a1=a1: e.tensor_tensor(
            a0[:], a1[:], a0[:], OP.subtract), waits=[t2, t3], inc="dve")
        rowA_free, rowB_free = t1, t3
        t5 = kb.emit("act", lambda e, a0=a0, a1=a1: e.activation(
            a1[:], a0[:], AF.Ln), waits=[t4], inc="act")
        t6 = kb.emit("act", lambda e, a1=a1, r_f=r_f: e.activation(
            r_f[:], a1[:], AF.Exp, scale=-0.5), waits=[t5], inc="act")
        ln2_rows[u] = (t1, t6)
        bm = kb.emit("pe", lambda e, m_f=m_f: e.matmul(
            pR0[:], onesrf_s[:], m_f[:], start=True, stop=True),
            waits=[t1, ln2_done.get(u - 1)], inc="pe")
        br = kb.emit("pe", lambda e, r_f=r_f: e.matmul(
            pR1[:], onesrf_s[:], r_f[:], start=True, stop=True),
            waits=[t6], inc="pe")
        tn = None
        for ch in range(2):
            c0 = ch * 4
            lv = lnt_s[:].rearrange("p (a b j) -> p a b j", b=4, j=128)
            bc0 = r3(pR0[:]).unsqueeze(2).broadcast_to([128, 4, 4, 128])
            bc1 = r3(pR1[:]).unsqueeze(2).broadcast_to([128, 4, 4, 128])
            tt1 = kb.emit("dve", lambda e, u=u, c0=c0, lv=lv, bc0=bc0: e.tensor_tensor(
                lv, xT2[:, 4 * u:4 * u + 4, c0:c0 + 4, :], bc0, OP.subtract),
                waits=[bm, br], inc="dve")
            tn = kb.emit("dve", lambda e, u=u, c0=c0, lv=lv, bc1=bc1: e.tensor_tensor(
                big_s[:, 4 * u:4 * u + 4, c0:c0 + 4, :], lv, bc1, OP.mult),
                waits=[tt1], inc="dve")
        gc_last = kb.emit("dve", lambda e, u=u: e.tensor_reduce(
            gcp_s[:, u * 8:(u + 1) * 8],
            big_s[:, 4 * u:4 * u + 4, :, :].transpose([0, 2, 1, 3]),
            AX.XY, OP.add), waits=[tn], inc="dve")
        ln2_done[u] = tn
    tgs = gc_last
    for u in range(1, NU):
        tgs = kb.emit("dve", lambda e, u=u: e.tensor_tensor(
            gcp_s[:, 0:8], gcp_s[:, 0:8], gcp_s[:, u * 8:u * 8 + 8], OP.add),
            waits=[tgs], inc="dve")
    tgr = kb.emit("dve", lambda e: e.tensor_scalar(
        gcr_s[:], gcp_s[:, 0:8], 1.0 / L, None, OP.mult), waits=[tgs], inc="dve")
    gcw_tok = wq_casts["gc2b"]
    mms = []
    for m in range(4):
        mm = None
        for kc in range(NCH):
            mm = kb.emit("pe", lambda e, m=m, kc=kc: e.matmul(
                pO0[:, m:m + 1], gc1b[:, kc, m * 128:(m + 1) * 128],
                gcr_s[:, kc:kc + 1], start=(kc == 0), stop=(kc == NCH - 1)),
                waits=[tgr, gcw_tok], inc="pe")
        mms.append(mm)
    g1 = []
    for m in range(4):
        g1.append(kb.emit("act", lambda e, m=m: e.activation(
            gch1_s[:, m:m + 1], pO0[:, m:m + 1], AF.Gelu,
            bias=gcb1_s[:, m:m + 1]), waits=[mms[-1]], inc="act"))
    mms = []
    for m in range(2):
        mm = None
        for kc in range(4):
            mm = kb.emit("pe", lambda e, m=m, kc=kc: e.matmul(
                pO1[:, m:m + 1], gc2b[:, kc, m * 128:(m + 1) * 128],
                gch1_s[:, kc:kc + 1], start=(kc == 0), stop=(kc == 3)),
                waits=[g1[-1]], inc="pe")
        mms.append(mm)
    g2 = []
    for m in range(2):
        g2.append(kb.emit("act", lambda e, m=m: e.activation(
            gch2_s[:, m:m + 1], pO1[:, m:m + 1], AF.Gelu,
            bias=gcb2_s[:, m:m + 1]), waits=[mms[-1]], inc="act"))
    gc_pO_tok = (g1[-1], g2[-1])
    mm = None
    for kc in range(2):
        mm = kb.emit("pe", lambda e, kc=kc: e.matmul(
            pRowA[0:1, 0:1], gcw3_s[:, kc:kc + 1], gch2_s[:, kc:kc + 1],
            start=(kc == 0), stop=(kc == 1)), waits=[g2[-1]], inc="pe")
    tth_gc = kb.emit("act", lambda e: e.activation(
        scal_s[0:1, 3:4], pRowA[0:1, 0:1], AF.Tanh, scale=0.5,
        bias=scal_s[0:1, 1:2]), waits=[mm], inc="act")
    tt = kb.emit("dve", lambda e: e.tensor_tensor(
        scal_s[0:1, 7:8], scal_s[0:1, 2:3], scal_s[0:1, 3:4], OP.mult),
        waits=[tth_gc], inc="dve")
    c0_tok = kb.emit("dve", lambda e: e.tensor_scalar(
        scal_s[0:1, 4:5], scal_s[0:1, 7:8], 0.7, 0.1, OP.mult, OP.add),
        waits=[tt], inc="dve")
    c1_tok = kb.emit("dve", lambda e: e.tensor_scalar(
        scal_s[0:1, 5:6], scal_s[0:1, 2:3], 0.3, None, OP.mult), inc="dve")
    outT2 = (outT, sb("outT2", [128, 8, 256], F32))
    pC_free = [None, None]
    pCP = (pR0, pR1)
    pC4 = (pRowA, pRowB)
    cp_free = [ln2_done[NU - 1], ln2_done[NU - 1]]
    exit_free = [gc_pO_tok[0], gc_pO_tok[1]]
    c4_free = [tth_gc, rowB_free]
    prev_tcp = [None, None]
    osc_free = [None, None]
    last_twists = [None, None]
    for fp in range(0, NF, 2):
        out0_tok = [None, None]
        for p in range(2):
            f = fp + p
            t0 = f * 2
            m_f = rows_s[0:1, f * 256:(f + 1) * 256]
            r_f = rows_s[0:1, 2048 + f * 256:2048 + (f + 1) * 256]
            rt = ln2_rows[f // 2]
            bm = kb.emit("pe", lambda e, m_f=m_f: e.matmul(
                pR0[:, 0:256], onesrf_s[:], m_f[:], start=True, stop=True),
                waits=[cp_free[0], out0_tok[0] if p else None, rt[0], rt[1],
                       last_twists[0], last_twists[1]], inc="pe")
            br = kb.emit("pe", lambda e, r_f=r_f: e.matmul(
                pR1[:, 0:256], onesrf_s[:], r_f[:], start=True, stop=True),
                waits=[cp_free[1]], inc="pe")
            lv = lnt_s[:].rearrange("p (c a j) -> p c a j", a=2, j=128)
            bc0 = pR0[:, 0:256].rearrange("p (a j) -> p a j", j=128).unsqueeze(
                1).broadcast_to([128, 8, 2, 128])
            bc1 = pR1[:, 0:256].rearrange("p (a j) -> p a j", j=128).unsqueeze(
                1).broadcast_to([128, 8, 2, 128])
            tt1 = kb.emit("dve", lambda e, t0=t0, lv=lv, bc0=bc0: e.tensor_tensor(
                lv, xT2[:, t0:t0 + 2, :, :].transpose([0, 2, 1, 3]), bc0,
                OP.subtract), waits=[bm, br], inc="dve")
            tn = kb.emit("dve", lambda e, p=p, lv=lv, bc1=bc1: e.tensor_tensor(
                outT2[p][:, :, :].rearrange("p c (a j) -> p c a j", j=128),
                lv, bc1, OP.mult), waits=[tt1], inc="dve")
            out0_tok[p] = tn
        twist_tok = [out0_tok[0], out0_tok[1]]
        for cyc in range(3):
            for p in range(2):
                f = fp + p
                t0 = f * 2
                oT = outT2[p]
                g1 = []
                for m in range(NCH):
                    yi = m % 2
                    pC = (pA, pB)[yi]
                    mm = None
                    for kc in range(NCH):
                        mm = kb.emit("pe", lambda e, pC=pC, m=m, kc=kc, t0=t0: e.matmul(
                            pC[:, 0:256], cn1b[:, kc, m * 128:(m + 1) * 128],
                            fms(big_s, t0, 2, kc),
                            start=(kc == 0), stop=(kc == NCH - 1)),
                            waits=[ffnw_tok, pC_free[yi], twist_tok[p],
                                   ln2_done[t0 // 4] if cyc == 0 else None], inc="pe")
                    g1.append(kb.emit("act", lambda e, pC=pC, m=m: e.activation(
                        h1b[:, m, :], pC[:, 0:256], AF.Gelu, bias=cnb1_s[:, m:m + 1]),
                        waits=[mm], inc="act"))
                    pC_free[yi] = g1[-1]
                g2 = []
                for m in range(4):
                    yi = m % 2
                    pC = (pA, pB)[yi]
                    mm = None
                    for kc in range(NCH):
                        mm = kb.emit("pe", lambda e, pC=pC, m=m, kc=kc: e.matmul(
                            pC[:, 0:256], cn2b[:, kc, m * 128:(m + 1) * 128],
                            h1b[:, kc, :], start=(kc == 0), stop=(kc == NCH - 1)),
                            waits=[g1[kc], pC_free[yi]], inc="pe")
                    g2.append(kb.emit("act", lambda e, pC=pC, m=m: e.activation(
                        h2b[:, m, :], pC[:, 0:256], AF.Gelu, bias=cnb2_s[:, m:m + 1]),
                        waits=[mm], inc="act"))
                    pC_free[yi] = g2[-1]
                g3 = []
                for m in range(2):
                    yi = m % 2
                    pC = (pA, pB)[yi]
                    mm = None
                    for kc in range(4):
                        mm = kb.emit("pe", lambda e, pC=pC, m=m, kc=kc: e.matmul(
                            pC[:, 0:256], cn3b[:, kc, m * 128:(m + 1) * 128],
                            h2b[:, kc, :], start=(kc == 0), stop=(kc == 3)),
                            waits=[g2[kc], pC_free[yi]], inc="pe")
                    g3.append(kb.emit("act", lambda e, pC=pC, m=m: e.activation(
                        h3b[:, m, :], pC[:, 0:256], AF.Gelu, bias=cnb3_s[:, m:m + 1]),
                        waits=[mm], inc="act"))
                    pC_free[yi] = g3[-1]
                mm = None
                for kc in range(2):
                    mm = kb.emit("pe", lambda e, kc=kc, p=p: e.matmul(
                        pC4[p][0:1, 0:256], cnw4_s[:, kc:kc + 1], h3b[:, kc, :],
                        start=(kc == 0), stop=(kc == 1)),
                        waits=[g3[kc], c4_free[p]], inc="pe")
                a_th = aux_s[0:1, p * 256:(p + 1) * 256]
                a_cp = aux_s[0:1, 512 + p * 256:512 + (p + 1) * 256]
                tth = kb.emit("act", lambda e, p=p, a_th=a_th: e.activation(
                    a_th[:], pC4[p][0:1, 0:256], AF.Tanh, scale=0.5,
                    bias=scal_s[0:1, 0:1]), waits=[mm, prev_tcp[p]], inc="act")
                c4_free[p] = tth
                tcp = kb.emit("dve", lambda e, a_th=a_th, a_cp=a_cp: e.tensor_scalar(
                    a_cp[:], a_th[:], scal_s[0:1, 5:6],
                    scal_s[0:1, 4:5], OP.mult, OP.add),
                    waits=[tth, c0_tok, c1_tok], inc="dve")
                prev_tcp[p] = tcp
                tbc = kb.emit("pe", lambda e, p=p, a_cp=a_cp: e.matmul(
                    pCP[p][:, 0:256], onesrf_s[:], a_cp[:],
                    start=True, stop=True), waits=[tcp, cp_free[p]], inc="pe")
                lA = lnt_s[:, 0:1024].rearrange("p (c f) -> p c f", f=256)
                lB = lnt_s[:, 1024:2048].rearrange("p (c f) -> p c f", f=256)
                cbc = pCP[p][:, 0:256].unsqueeze(1).broadcast_to([128, 4, 256])
                tb1 = kb.emit("dve", lambda e, oT=oT, lA=lA, cbc=cbc: e.tensor_tensor(
                    lA, oT[:, 4:8, :], cbc, OP.mult), waits=[tbc], inc="dve")
                tb2 = kb.emit("dve", lambda e, oT=oT, lB=lB, cbc=cbc: e.tensor_tensor(
                    lB, oT[:, 0:4, :], cbc, OP.mult), inc="dve")
                kb.emit("dve", lambda e, oT=oT, lA=lA: e.tensor_tensor(
                    oT[:, 0:2, :], oT[:, 0:2, :], lA[:, 0:2, :], OP.add),
                    waits=[tb1], inc="dve")
                kb.emit("dve", lambda e, oT=oT, lA=lA: e.tensor_tensor(
                    oT[:, 2:4, :], oT[:, 2:4, :], lA[:, 2:4, :], OP.subtract),
                    inc="dve")
                kb.emit("dve", lambda e, oT=oT, lB=lB: e.tensor_tensor(
                    oT[:, 4:6, :], oT[:, 4:6, :], lB[:, 0:2, :], OP.subtract),
                    waits=[tb2], inc="dve")
                tw = kb.emit("dve", lambda e, oT=oT, lB=lB: e.tensor_tensor(
                    oT[:, 6:8, :], oT[:, 6:8, :], lB[:, 2:4, :], OP.add),
                    inc="dve")
                cp_free[p] = tw
                twist_tok[p] = tw
                if cyc < 2:
                    tcast = kb.emit("dve", lambda e, t0=t0, oT=oT: e.tensor_copy(
                        big_s[:, t0:t0 + 2, :, :],
                        oT[:, :, :].rearrange("p c (a j) -> p c a j", j=128
                                              ).transpose([0, 2, 1, 3])),
                        waits=[tw, g1[-1]], inc="dve")
                    twist_tok[p] = tcast
        last_twists = [twist_tok[0], twist_tok[1]]
        for p in range(2):
            f = fp + p
            t0 = f * 2
            oT = outT2[p]
            tadd = kb.emit("dve", lambda e, t0=t0, oT=oT: e.tensor_tensor(
                xT2[:, t0:t0 + 2, :, :], xT2[:, t0:t0 + 2, :, :],
                oT[:, :, :].rearrange("p c (a j) -> p c a j", j=128
                                      ).transpose([0, 2, 1, 3]), OP.add),
                waits=[twist_tok[p]], inc="dve")
            last_twists[p] = tadd
            for s in range(2):
                t = t0 + s
                ob = osc[t % 2]
                tv = None
                for half in range(2):
                    pt_ = (pO0, pO1)[half]
                    tp = None
                    for cc in range(4):
                        c = half * 4 + cc
                        tp = kb.emit("pe", lambda e, pt_=pt_, cc=cc, t=t, c=c: e.transpose(
                            pt_[:, cc * 128:(cc + 1) * 128], xT2[:, t, c, :], ident_s[:]),
                            waits=[tadd, exit_free[half], osc_free[t % 2]], inc="pe")
                    tv = kb.emit("act", lambda e, pt_=pt_, ob=ob, half=half: e.activation(
                        ob[:, half * 512:(half + 1) * 512], pt_[:], AF.Copy),
                        waits=[tp], inc="act")
                    exit_free[half] = tv
                td = kb.emit("po", lambda e, ob=ob, t=t: e.dma_start(
                    out_t[t * 128:(t + 1) * 128, :], ob),
                    waits=[tv], inc="dO%d" % (t % 2), inc_n=16)
                osc_free[t % 2] = td
    kb.wait_only("po", [kb.tok("dO0"), kb.tok("dO1")])
    kb.run("ph45")

    for eng in ENGS:
        for sem, val in kb.waited[eng].items():
            assert val <= kb.c[sem], (eng, sem, val, kb.c[sem])
    for name in list(guards):
        try:
            guards.pop(name).__exit__(None, None, None)
        except Exception:
            pass
    return nc


def prep_inputs(inp):
    f = lambda a: np.ascontiguousarray(np.asarray(a, dtype=np.float32))
    cc = np.ascontiguousarray
    d = {}
    d["w_in"] = f(inp["L0_attn_w_in"])
    d["w_out"] = f(inp["L0_attn_w_out"])
    d["b_qkv"] = cc(f(inp["L0_attn_b_in"]).reshape(24, 128).T)
    d["b_out_c"] = cc(f(inp["L0_attn_b_out"]).reshape(8, 128).T)
    d["cn_w1"] = f(inp["L0_ffn_cn_w1"])
    d["cn_w2"] = f(inp["L0_ffn_cn_w2"])
    d["cn_w3"] = f(inp["L0_ffn_cn_w3"])
    d["cn_w4c"] = cc(f(inp["L0_ffn_cn_w4"]).reshape(2, 128).T)
    d["cn_b1c"] = cc(f(inp["L0_ffn_cn_b1"]).reshape(8, 128).T)
    d["cn_b2c"] = cc(f(inp["L0_ffn_cn_b2"]).reshape(4, 128).T)
    d["cn_b3c"] = cc(f(inp["L0_ffn_cn_b3"]).reshape(2, 128).T)
    d["cn_b4"] = f(inp["L0_ffn_cn_b4"]).reshape(1, 1)
    d["gc_w1"] = f(inp["L0_ffn_gc_w1"])
    d["gc_w2"] = f(inp["L0_ffn_gc_w2"])
    d["gc_w3c"] = cc(f(inp["L0_ffn_gc_w3"]).reshape(2, 128).T)
    d["gc_b1c"] = cc(f(inp["L0_ffn_gc_b1"]).reshape(4, 128).T)
    d["gc_b2c"] = cc(f(inp["L0_ffn_gc_b2"]).reshape(2, 128).T)
    d["gc_b3"] = f(inp["L0_ffn_gc_b3"]).reshape(1, 1)
    d["adrange"] = f(inp["L0_ffn_adaptive_range"]).reshape(1, 1)
    return d


def kernel(**inputs):
    x = np.asarray(inputs["L0_attn_x"], dtype=np.float32)
    B, L, _ = x.shape
    shared = prep_inputs(inputs)
    nc = build_core_program(L=L)
    in_maps = []
    for b in range(B):
        m = dict(shared)
        m["x"] = np.ascontiguousarray(x[b])
        in_maps.append(m)
    res = run_bass_kernel_spmd(nc, in_maps, core_ids=list(range(B)))
    return np.stack([np.asarray(r["out"]) for r in res.results], axis=0)
